# version 07
# baseline (speedup 1.0000x reference)
import os
import numpy as np
import ml_dtypes
import concourse.bass as bass
import concourse.mybir as mybir
from concourse.bass_utils import run_bass_kernel_spmd

SIM_DRAIN = (os.environ.get("BASS_SIM_FP64") == "1"
             or os.environ.get("KERNEL_FORCE_DRAIN") == "1")


_DR_ENG = os.environ.get("KERNEL_DRAIN_ENG", "")
_DR_TAGS = set(os.environ.get("KERNEL_DRAIN_TAGS", "").split(","))


def _dr(e, tag="aux"):
    if SIM_DRAIN or (_DR_ENG and _DR_ENG in type(e).__name__)             or tag in _DR_TAGS:
        e.drain()

F32 = mybir.dt.float32
BF16 = mybir.dt.bfloat16
FP8 = mybir.dt.float8e5
AF = mybir.ActivationFunctionType
OP = mybir.AluOpType

LN2 = float(np.log(2.0))
LN2_HI = float(np.float32(np.log(2.0)))
NEG_D = float(LN2 - LN2_HI)
INV_LN2 = float(1.0 / np.log(2.0))
DELTA = 3e-9
WBAND = 2.0 ** -6
SLACK_H = 2.0 ** -22 + 2.0 ** -35
SLACK_Y2 = 2.0 ** -17 + 2.0 ** -29
USE_DR = os.environ.get("KERNEL_NO_DR") != "1"
PIPELINE = os.environ.get("KERNEL_NO_PIPE") != "1"


class Cfg:
    def __init__(self, T=4096, F1=4096, p3_chunks=None):
        self.D = 1024
        self.T = T
        self.F1 = F1
        self.NC = T // 512
        self.NKT = T // 128
        self.NF1T = F1 // 128
        self.DC = 8
        self.p3_chunks = self.NC if p3_chunks is None else p3_chunks


class Sem:
    def __init__(self, es, nc, name):
        self.h = es.enter_context(nc.semaphore(name))
        self.v = 0

    def inc(self, n=1):
        self.v += n
        return self.v


class DQ:

    def __init__(self, es, nc, prog, stream, name, k=4):
        self.sems = [Sem(es, nc, f"{name}{i}") for i in range(k)]
        self.k = k
        self.n = 0
        self.prog = prog
        self.stream = stream

    def issue(self, out, in_, waits=()):
        i = self.n % self.k
        self.n += 1
        s = self.sems[i]
        pre = s.v
        tgt = s.inc(16)

        def fn(e, out=out, in_=in_, waits=tuple(waits), s=s, pre=pre):
            for (h, v) in waits:
                e.wait_ge(h, v)
            if pre:
                e.wait_ge(s.h, pre)
            e.dma_start(out=out, in_=in_).then_inc(s.h, 16)
        self.prog.on(self.stream, fn)
        return (s.h, tgt)

    def all_done(self):
        return [(s.h, s.v) for s in self.sems if s.v]


class Prog:
    ENGS = ("sync", "scalar", "vector", "gpsimd", "tensor")

    def __init__(self):
        self.items = {e: [] for e in self.ENGS}

    def on(self, eng, fn):
        self.items[eng].append(fn)


def _w(e, waits):
    for (h, v) in waits:
        e.wait_ge(h, v)



def emit_quantize(bld, items, M, emin, signed, pre=(), band=0.0, slack=DELTA,
                  ulev=5):
    prog = bld["prog"]
    SA, SV = bld["SA"], bld["SV"]
    tmp = bld["tmp"]
    hist = bld["tmp_hist"]
    CST = bld["CST"]
    pre = tuple(pre) + (bld["w_cst"],)
    outs = []
    for i0 in range(0, len(items), 2):
        pair = items[i0:i0 + 2]
        for j, it in enumerate(pair):
            x = it["in"]
            fd = x.shape[-1]
            T_, A, SG = tmp["T"][j][:, :fd], tmp["A"][j][:, :fd], tmp["SG"][j][:, :fd]
            hw = hist[j]

            def a1(s, x=x, T_=T_, A=A, SG=SG, hw=hw, pre=tuple(pre)):
                _w(s, pre)
                if hw:
                    _w(s, [hw])
                _dr(s, "chs")
                if signed:
                    s.activation(out=T_, in_=x, func=AF.Abs)
                    _dr(s, "chs")
                    s.activation(out=SG, in_=x, func=AF.Sign)
                    _dr(s, "chs")
                    s.activation(out=A, in_=T_, func=AF.Ln, bias=CST[1e-30])
                else:
                    s.activation(out=A, in_=x, func=AF.Ln, bias=CST[1e-30])
                _dr(s, "chs")
                s.activation(out=A, in_=A, func=AF.Identity, scale=INV_LN2) \
                    .then_inc(SA.h, 1)
                _dr(s, "chs")
            prog.on("scalar", a1)
            it["a1"] = SA.inc()
        for j, it in enumerate(pair):
            fd = it["in"].shape[-1]
            A, B, G = (tmp["A"][j][:, :fd], tmp["B"][j][:, :fd],
                       tmp["G"][j][:, :fd])

            def v1(v, w=it["a1"], A=A, B=B, G=G):
                v.wait_ge(SA.h, w)
                _dr(v, "chv")
                v.tensor_scalar(out=B, in0=A, scalar1=float(emin),
                                scalar2=float(-emin), op0=OP.max, op1=OP.add)
                _dr(v, "chv")
                src = B
                for lv in (2 ** p for p in range(ulev - 1, -1, -1)):
                    v.tensor_scalar(out=G, in0=src, scalar1=float(lv),
                                    scalar2=float(-lv), op0=OP.is_ge,
                                    op1=OP.mult)
                    _dr(v, "chv")
                    v.tensor_add(out=A, in0=src, in1=G)
                    _dr(v, "chv")
                    src = A
                v.scalar_tensor_tensor(out=B, in0=B, scalar=float(emin - M),
                                       in1=A, op0=OP.add, op1=OP.subtract)
                _dr(v, "chv")
                v.tensor_scalar(out=A, in0=B, scalar1=LN2_HI, scalar2=None,
                                op0=OP.mult)
                _dr(v, "chv")
                v.scalar_tensor_tensor(out=A, in0=B, scalar=NEG_D, in1=A,
                                       op0=OP.mult, op1=OP.add) \
                    .then_inc(SV.h, 1)
                _dr(v, "chv")
            prog.on("vector", v1)
            it["v1"] = SV.inc()
        for j, it in enumerate(pair):
            fd = it["in"].shape[-1]
            A = tmp["A"][j][:, :fd]
            ISB, SRB = tmp["ISB"][j][:, :fd], tmp["SRB"][j][:, :fd]

            def a2(s, w=it["v1"], A=A, ISB=ISB, SRB=SRB):
                s.wait_ge(SV.h, w)
                _dr(s, "chs")
                s.activation(out=ISB, in_=A, func=AF.Exp, scale=-1.0)
                _dr(s, "chs")
                s.activation(out=SRB, in_=A, func=AF.Exp) \
                    .then_inc(SA.h, 1)
                _dr(s, "chs")
            prog.on("scalar", a2)
            it["a2"] = SA.inc()
        for j, it in enumerate(pair):
            x = it["in"]
            fd = x.shape[-1]
            T_, A, B, G, SG = (tmp["T"][j][:, :fd], tmp["A"][j][:, :fd],
                               tmp["B"][j][:, :fd], tmp["G"][j][:, :fd],
                               tmp["SG"][j][:, :fd])
            ISB, SRB = tmp["ISB"][j][:, :fd], tmp["SRB"][j][:, :fd]

            def v2(v, w=it["a2"], out=it["out"], x=x, T_=T_, A=A, B=B, G=G,
                   SG=SG, ISB=ISB, SRB=SRB):
                v.wait_ge(SA.h, w)
                _dr(v, "chv")
                v.tensor_mul(out=A, in0=(T_ if signed else x), in1=ISB)
                _dr(v, "chv")
                src = A
                for lv in (2 ** p for p in range(M, 0, -1)):
                    v.tensor_scalar(out=G, in0=src, scalar1=float(lv),
                                    scalar2=float(-lv), op0=OP.is_ge,
                                    op1=OP.mult)
                    _dr(v, "chv")
                    v.tensor_add(out=B, in0=src, in1=G)
                    _dr(v, "chv")
                    src = B
                v.tensor_scalar(out=G, in0=B, scalar1=float(0.5 + band),
                                scalar2=slack, op0=OP.subtract,
                                op1=OP.is_gt)
                _dr(v, "chv")
                v.tensor_scalar(out=T_, in0=B, scalar1=float(1.5 - band),
                                scalar2=-slack, op0=OP.subtract,
                                op1=OP.is_ge)
                _dr(v, "chv")
                v.tensor_add(out=G, in0=G, in1=T_)
                _dr(v, "chv")
                v.tensor_sub(out=A, in0=A, in1=B)
                _dr(v, "chv")
                v.tensor_add(out=A, in0=A, in1=G)
                _dr(v, "chv")
                if signed:
                    v.tensor_mul(out=A, in0=A, in1=SRB)
                    _dr(v, "chv")
                    v.tensor_mul(out=out, in0=A, in1=SG).then_inc(SV.h, 1)
                else:
                    v.tensor_mul(out=out, in0=A, in1=SRB).then_inc(SV.h, 1)
                _dr(v, "chv")
            prog.on("vector", v2)
            it["v2"] = SV.inc()
            hist[j] = (SV.h, it["v2"])
            outs.append((SV.h, it["v2"]))
    return outs



def build_nc(cfg: Cfg):
    import contextlib
    nc = bass.Bass("TRN2", target_bir_lowering=False, debug=False)
    es = contextlib.ExitStack()
    D, T, F1 = cfg.D, cfg.T, cfg.F1
    NCH, NKT, NF1T, DC = cfg.NC, cfg.NKT, cfg.NF1T, cfg.DC
    NQK = 16

    xT = nc.dram_tensor("xT", [NCH, D, 512], F32, kind="ExternalInput").ap()
    wqkvT = nc.dram_tensor("wqkvT", [D, 3 * D], F32, kind="ExternalInput").ap()
    bqk = nc.dram_tensor("bqk", [128, NQK], F32, kind="ExternalInput").ap()
    bvrow = nc.dram_tensor("bvrow", [1, D], F32, kind="ExternalInput").ap()
    w1T = nc.dram_tensor("w1T", [D, F1], F32, kind="ExternalInput").ap()
    w2T = nc.dram_tensor("w2T", [F1, D], F32, kind="ExternalInput").ap()
    b1r = nc.dram_tensor("b1r", [128, NF1T], F32, kind="ExternalInput").ap()
    b2r = nc.dram_tensor("b2r", [128, DC], F32, kind="ExternalInput").ap()
    maskT = nc.dram_tensor("maskT", [NCH, D, 512], BF16, kind="ExternalInput").ap()
    outT = nc.dram_tensor("outT", [NCH, D, 512], F32, kind="ExternalOutput").ap()
    qT_s = nc.dram_tensor("qT_s", [NCH, D, 512], F32, kind="Internal").ap()
    kT_s = nc.dram_tensor("kT_s", [NKT, D, 128], F32, kind="Internal").ap()
    v_s = nc.dram_tensor("v_s", [T, D], F32, kind="Internal").ap()
    xn8_s = nc.dram_tensor("xn8_s", [NCH, 128, DC * 512], FP8, kind="Internal").ap()
    w1q_s = nc.dram_tensor("w1q_s", [NF1T, 128, DC * 128], FP8, kind="Internal").ap()
    w2q_s = nc.dram_tensor("w2q_s", [NF1T, 128, D], FP8, kind="Internal").ap()

    prog = Prog()
    ST = Sem(es, nc, "st_t")
    SV = Sem(es, nc, "st_v")
    SA = Sem(es, nc, "st_a")
    qload = DQ(es, nc, prog, "sync", "ql", k=6)
    qspill = DQ(es, nc, prog, "gpsimd", "qs", k=4)
    qstore = DQ(es, nc, prog, "gpsimd", "qo", k=2)

    sbuf = lambda name, shape, dt: es.enter_context(nc.sbuf_tensor(name, shape, dt))

    NQ4 = 4
    QTL = NKT // NQ4
    PBW = max(2 * QTL * 512, DC * 512)
    CARVE = (PBW + 2 * DC * 512 + DC * 512 + DC * 512 +
             2 * 512 + 2 * 512 + 2 * 512 + 2 * DC * 128)
    arf = sbuf("arf", [128, CARVE], F32)
    off = 0

    def carve(n):
        nonlocal off
        a = arf[:, off:off + n]
        off += n
        return a
    Pbuf2 = carve(PBW)
    Pbuf_h = [Pbuf2[:, 0:QTL * 512], Pbuf2[:, QTL * 512:2 * QTL * 512]]
    s_acc2 = carve(2 * DC * 512)
    s_acc = [s_acc2[:, 0:DC * 512], s_acc2[:, DC * 512:2 * DC * 512]]
    qtc = carve(DC * 512)
    gbuf = carve(DC * 512)
    y1st = carve(2 * 512)
    y2st = carve(2 * 512)
    vtb = carve(2 * 512)
    ktb = carve(2 * DC * 128)
    assert off == CARVE
    xnc = Pbuf2[:, 0:DC * 512]
    sqb = s_acc[0][:, 0:2 * 512]
    stg = s_acc[0][:, 2 * 512:4 * 512]
    wqb = qtc[:, 0:2 * DC * 128]
    wvb = gbuf[:, 0:2 * 512]

    ones128 = sbuf("ones128", [128, 1], F32)
    ones1 = sbuf("ones1", [1, 128], F32)
    bqk_t = sbuf("bqk_t", [128, NQK], F32)
    b1_t = sbuf("b1_t", [128, NF1T], F32)
    b2_t = sbuf("b2_t", [128, DC], F32)
    rows = sbuf("rows", [1, 2 * 1024], F32)
    den_p = [rows[0:1, 0:512], rows[0:1, 1024:1536]]
    rcp_p = [rows[0:1, 512:1024], rows[0:1, 1536:2048]]
    bv_t = sbuf("bv_t", [1, D], F32)
    xcb = sbuf("xcb", [128, DC * 512], F32)
    xn8c = sbuf("xn8c", [128, DC * 512], FP8)
    w1qb = sbuf("w1qb", [128, 2 * DC * 128], FP8)
    w2qb = sbuf("w2qb", [128, 2 * NF1T * 128], FP8)
    h8 = sbuf("h8", [128, NF1T * 512], FP8)
    mkb = sbuf("mkb", [128, 2 * 512], BF16)

    QF = 512
    tmp = {}
    _gdt = F32
    for nm in ("T", "A", "B"):
        th = sbuf("tmq_" + nm, [128, 2 * QF], F32)
        tmp[nm] = [th[:, 0:QF], th[:, QF:2 * QF]]
    for nm in ("G", "SG"):
        th = sbuf("tmq_" + nm, [128, 2 * QF], _gdt)
        tmp[nm] = [th[:, 0:QF], th[:, QF:2 * QF]]
    for nm in ("ISB", "SRB"):
        th = sbuf("tmq_" + nm, [128, 2 * QF], BF16)
        tmp[nm] = [th[:, 0:QF], th[:, QF:2 * QF]]

    pb = [es.enter_context(nc.psum_tensor("pb%d" % i, [128, 512], F32))
          for i in range(8)]
    bank_free = [()] * 8

    _cvals = [1e-30, 1e-5]
    cst = sbuf("cst", [128, len(_cvals)], F32)
    cst_d = nc.inline_tensor(
        np.tile(np.asarray(_cvals, np.float32), (128, 1)), name="cstd")
    CST = {v: cst[:, i:i + 1] for i, v in enumerate(_cvals)}
    CST1 = {v: cst[0:1, i:i + 1] for i, v in enumerate(_cvals)}

    bld = {"nc": nc, "prog": prog, "SA": SA, "SV": SV, "tmp": tmp,
           "tmp_hist": [None, None], "CST": CST}

    def init_v(v):
        v.memset(ones128[:], 1.0)
        _dr(v)
        v.memset(ones1[:], 1.0).then_inc(SV.h, 1)
        _dr(v)
    prog.on("vector", init_v)
    t_ones = (SV.h, SV.inc())

    w_cst = qload.issue(cst[:], cst_d.ap())
    bld["w_cst"] = w_cst
    w_bqk = qload.issue(bqk_t[:], bqk[:])
    w_b1 = qload.issue(b1_t[:], b1r[:])
    w_b2 = qload.issue(b2_t[:], b2r[:])
    w_bv = qload.issue(bv_t[:], bvrow[:])
    w_b1q = emit_quantize(bld, [{"in": b1_t[:], "out": b1_t[:]}], M=2, emin=-14,
                          signed=True, pre=[w_b1])[-1]
    w_b2q = emit_quantize(bld, [{"in": b2_t[:], "out": b2_t[:]}], M=2, emin=-14,
                          signed=True, pre=[w_b2])[-1]

    wq_pieces = [("w1", i) for i in range(NF1T)] + [("w2", i) for i in range(NF1T)]
    wq_state = {"i": 0, "land_free": [(), ()]}

    def emit_wq_piece():
        if wq_state["i"] >= len(wq_pieces):
            return
        kind, a = wq_pieces[wq_state["i"]]
        slot = wq_state["i"] % 2
        wq_state["i"] += 1
        land = xcb[:, slot * DC * 128:(slot + 1) * DC * 128]
        lw = list(wq_state["land_free"][slot]) + list(wq_state.get("xcb_guard", ()))
        if kind == "w1":
            src = w1T[:, a * 128:(a + 1) * 128].rearrange("(kc p) f -> p kc f", p=128)
            dst_l = land.rearrange("p (kc f) -> p kc f", kc=DC)
            w_ld = qload.issue(dst_l, src, waits=lw)
            dst = w1qb[:, slot * DC * 128:(slot + 1) * DC * 128]
        else:
            src = w2T[a * 128:(a + 1) * 128, 0:D]
            w_ld = qload.issue(land, src, waits=lw)
            dst = w2qb[:, slot * NF1T * 128: slot * NF1T * 128 + D]
        items = []
        for half in range(2):
            items.append({"in": land[:, half * 512:(half + 1) * 512],
                          "out": dst[:, half * 512:(half + 1) * 512]})
        pre = [w_ld] + list(wq_state.setdefault("dst_free", [(), ()])[slot])
        outs = emit_quantize(bld, items, M=2, emin=-14, signed=True, pre=pre,
                             ulev=4)
        wq_state["land_free"][slot] = [(SA.h, SA.v)]
        if kind == "w1":
            sp = qspill.issue(w1q_s[a], dst, waits=[outs[-1]])
        else:
            sp = qspill.issue(w2q_s[a], dst, waits=[outs[-1]])
        wq_state.setdefault("dst_free", [(), ()])[slot] = [sp]
        return

    xcb_free, xnc_free, x8_free = (), (), ()
    sq_free = [(), ()]
    stg_free = [(), ()]
    wqb_free = [(), ()]
    wvb_free = [(), ()]
    xn8_spill = [None] * NCH
    for c in range(NCH):
        if c > 0:
            xcb_free = list(xcb_free) + [(SA.h, SA.v)]
        w_x = qload.issue(xcb[:].rearrange("p (dc t) -> p dc t", dc=DC),
                          xT[c].rearrange("(dc p) t -> p dc t", p=128),
                          waits=xcb_free)
        for dc in range(DC):
            slot = dc % 2

            def p1sq(s, dc=dc, slot=slot, w=w_x, bf=sq_free[slot]):
                s.wait_ge(w[0], w[1])
                _w(s, bf)
                _dr(s)
                s.activation(out=sqb[:, slot * 512:(slot + 1) * 512],
                             in_=xcb[:, dc * 512:(dc + 1) * 512],
                             func=AF.Square).then_inc(SA.h, 1)
                _dr(s)
            prog.on("scalar", p1sq)
            t_sq = SA.inc()

            def p1mm(te, dc=dc, slot=slot, wx=w_x, wsq=t_sq,
                     bf0=bank_free[0], bf1=bank_free[1], wo=t_ones):
                te.wait_ge(wx[0], wx[1])
                te.wait_ge(SA.h, wsq)
                if dc == 0:
                    te.wait_ge(wo[0], wo[1])
                    _w(te, bf0)
                    _w(te, bf1)
                te.matmul(pb[0][0:1, 0:512], ones128[:],
                          xcb[:, dc * 512:(dc + 1) * 512],
                          start=(dc == 0), stop=(dc == DC - 1))
                te.matmul(pb[1][0:1, 0:512], ones128[:],
                          sqb[:, slot * 512:(slot + 1) * 512],
                          start=(dc == 0), stop=(dc == DC - 1)).then_inc(ST.h, 1)
            prog.on("tensor", p1mm)
            t_mm = ST.inc()
            sq_free[slot] = [(ST.h, t_mm)]
        t_stats = ST.v
        mean = rows[0:1, 0:512]
        var = rows[0:1, 512:1024]
        rstd = rows[0:1, 1024:1536]

        def p1r1(v, w=t_stats):
            v.wait_ge(ST.h, w)
            _dr(v)
            v.tensor_single_scalar(out=mean, in_=pb[0][0:1, 0:512],
                                   scalar=1.0 / D, op=OP.mult)
            _dr(v)
            v.tensor_single_scalar(out=var, in_=pb[1][0:1, 0:512],
                                   scalar=1.0 / D, op=OP.mult)
            _dr(v)
            v.tensor_mul(out=rstd, in0=mean, in1=mean)
            _dr(v)
            v.tensor_sub(out=var, in0=var, in1=rstd).then_inc(SV.h, 1)
            _dr(v)
        prog.on("vector", p1r1)
        t_r1 = SV.inc()
        bank_free[0] = [(SV.h, t_r1)]
        bank_free[1] = [(SV.h, t_r1)]

        def p1r2(s, w=t_r1, wc=w_cst):
            s.wait_ge(SV.h, w)
            s.wait_ge(wc[0], wc[1])
            _dr(s)
            s.activation(out=var, in_=var, func=AF.Sqrt,
                         bias=CST1[1e-5]).then_inc(SA.h, 1)
            _dr(s)
        prog.on("scalar", p1r2)
        t_r2 = SA.inc()

        def p1r3(v, w=t_r2):
            v.wait_ge(SA.h, w)
            _dr(v)
            v.reciprocal(out=rstd, in_=var)
            _dr(v)
            v.tensor_mul(out=mean, in0=mean,
                         in1=rstd).then_inc(SV.h, 1)
            _dr(v)
        prog.on("vector", p1r3)
        t_r3 = SV.inc()

        def p1bc(te, w=t_r3, bf2=bank_free[2], bf3=bank_free[3]):
            te.wait_ge(SV.h, w)
            _w(te, bf2)
            _w(te, bf3)
            te.matmul(pb[2][:, 0:512], ones1[:], rstd, start=True, stop=True)
            te.matmul(pb[3][:, 0:512], ones1[:], mean, start=True, stop=True) \
              .then_inc(ST.h, 1)
        prog.on("tensor", p1bc)
        t_bc = ST.inc()
        for dc in range(DC):
            def p1xn(v, dc=dc, w=t_bc, xf=xnc_free if dc == 0 else ()):
                v.wait_ge(ST.h, w)
                _w(v, xf)
                _dr(v)
                v.tensor_mul(out=xnc[:, dc * 512:(dc + 1) * 512],
                             in0=xcb[:, dc * 512:(dc + 1) * 512],
                             in1=pb[2][:, 0:512])
                _dr(v)
                v.tensor_sub(out=xnc[:, dc * 512:(dc + 1) * 512],
                             in0=xnc[:, dc * 512:(dc + 1) * 512],
                             in1=pb[3][:, 0:512]).then_inc(SV.h, 1)
                _dr(v)
            prog.on("vector", p1xn)
            SV.inc()
        t_xn = SV.v
        bank_free[2] = [(SV.h, t_xn)]
        bank_free[3] = [(SV.h, t_xn)]
        xcb_free = [(SV.h, t_xn), (SA.h, SA.v), (ST.h, t_stats)]
        wq_state["xcb_guard"] = list(xcb_free)
        items = [{"in": xnc[:, dc * 512:(dc + 1) * 512],
                  "out": xn8c[:, dc * 512:(dc + 1) * 512]} for dc in range(DC)]
        pre = [(SV.h, t_xn)] + list(x8_free)
        q_outs = emit_quantize(bld, items, M=2, emin=-14, signed=True, pre=pre)
        xn8_spill[c] = qspill.issue(xn8_s[c], xn8c[:], waits=[q_outs[-1]])
        x8_free = [xn8_spill[c]]

        for ft in range(NQK):
            slot = ft % 2
            w_w = qload.issue(wqb[:, slot * DC * 128:(slot + 1) * DC * 128]
                              .rearrange("p (kc f) -> p kc f", kc=DC),
                              wqkvT[:, ft * 128:(ft + 1) * 128]
                              .rearrange("(kc p) f -> p kc f", p=128),
                              waits=wqb_free[slot])
            bank = 4 + ft % 2

            def p2mm(te, ft=ft, slot=slot, bank=bank, ww=w_w, wxn=t_xn,
                     bf=bank_free[bank]):
                te.wait_ge(ww[0], ww[1])
                te.wait_ge(SV.h, wxn)
                _w(te, bf)
                for dc in range(DC):
                    mm = te.matmul(pb[bank][:, 0:512],
                                   wqb[:, slot * DC * 128 + dc * 128:
                                       slot * DC * 128 + (dc + 1) * 128],
                                   xnc[:, dc * 512:(dc + 1) * 512],
                                   start=(dc == 0), stop=(dc == DC - 1))
                mm.then_inc(ST.h, 1)
            prog.on("tensor", p2mm)
            t_mm = ST.inc()
            wqb_free[slot] = [(ST.h, t_mm)]
            sslot = ft % 2

            def p2ev(s, ft=ft, bank=bank, sslot=sslot, w=t_mm, w0=w_bqk,
                     sf=stg_free[sslot]):
                s.wait_ge(ST.h, w)
                s.wait_ge(w0[0], w0[1])
                _w(s, sf)
                _dr(s)
                s.activation(out=stg[:, sslot * 512:(sslot + 1) * 512],
                             in_=pb[bank][:, 0:512], func=AF.Identity,
                             bias=bqk_t[:, ft:ft + 1]).then_inc(SA.h, 1)
                _dr(s)
            prog.on("scalar", p2ev)
            t_ev = SA.inc()
            bank_free[bank] = [(SA.h, t_ev)]
            if ft < 8:
                dst = qT_s[c][ft * 128:(ft + 1) * 128, :]
                src = stg[:, sslot * 512:(sslot + 1) * 512]
            else:
                f2 = ft - 8
                dst = kT_s[c * 4:(c + 1) * 4, f2 * 128:(f2 + 1) * 128, :] \
                    .rearrange("j d f -> d j f")
                src = stg[:, sslot * 512:(sslot + 1) * 512] \
                    .rearrange("p (j f) -> p j f", j=4)
            sp = qspill.issue(dst, src, waits=[(SA.h, t_ev)])
            stg_free[sslot] = [sp]
            if ft % 2 == 1:
                emit_wq_piece()
        for dvc in range(2):
            for dc in range(DC):
                slot = dc % 2
                w_wv = qload.issue(
                    wvb[:, slot * 512:(slot + 1) * 512],
                    wqkvT[dc * 128:(dc + 1) * 128, 2 * D + dvc * 512:
                          2 * D + (dvc + 1) * 512],
                    waits=wvb_free[slot])

                def p2vmm(te, dvc=dvc, dc=dc, slot=slot, ww=w_wv, wxn=t_xn,
                          wb=w_bv,
                          bfs=[list(bank_free[4 + b]) for b in range(4)]
                          if dc == 0 else None):
                    te.wait_ge(ww[0], ww[1])
                    if dc == 0:
                        te.wait_ge(SV.h, wxn)
                        te.wait_ge(wb[0], wb[1])
                        for bf in bfs:
                            _w(te, bf)
                    for tt in range(4):
                        mm = te.matmul(pb[4 + tt][:, 0:512],
                                       xnc[:, dc * 512 + tt * 128:
                                           dc * 512 + (tt + 1) * 128],
                                       wvb[:, slot * 512:(slot + 1) * 512],
                                       start=(dc == 0), stop=False)
                    if dc == DC - 1:
                        for tt in range(4):
                            mm = te.matmul(pb[4 + tt][:, 0:512], ones1[:],
                                           bv_t[0:1, dvc * 512:(dvc + 1) * 512],
                                           start=False, stop=True)
                    mm.then_inc(ST.h, 1)
                prog.on("tensor", p2vmm)
                t_mmv = ST.inc()
                wvb_free[slot] = [(ST.h, t_mmv)]
            t_vdone = ST.v
            for tt in range(4):
                sslot = tt % 2

                def p2vev(v, tt=tt, sslot=sslot, w=t_vdone, sf=stg_free[sslot]):
                    v.wait_ge(ST.h, w)
                    _w(v, sf)
                    _dr(v)
                    v.tensor_copy(out=stg[:, sslot * 512:(sslot + 1) * 512],
                                  in_=pb[4 + tt][:, 0:512]).then_inc(SV.h, 1)
                    _dr(v)
                prog.on("vector", p2vev)
                t_ev = SV.inc()
                bank_free[4 + tt] = [(SV.h, t_ev)]
                sp = qspill.issue(
                    v_s[(c * 4 + tt) * 128:(c * 4 + tt + 1) * 128,
                        dvc * 512:(dvc + 1) * 512],
                    stg[:, sslot * 512:(sslot + 1) * 512], waits=[(SV.h, t_ev)])
                stg_free[sslot] = [sp]
        xnc_free = [(ST.h, ST.v), (SV.h, SV.v), (SA.h, SA.v)]
        emit_wq_piece()

    while wq_state["i"] < len(wq_pieces):
        emit_wq_piece()
    t_wq_done = (SV.h, SV.v)
    p12_done = qspill.all_done()
    xcb_free = list(xcb_free)
    xnc_free = list(xnc_free) + [(SA.h, SA.v)]

    NP3 = cfg.p3_chunks
    NG = NF1T // NQ4
    qtc_free, xc_free = (), ()
    mk_free = [(), ()]
    ktb_free = [(), ()]
    vtb_free = [(), ()]
    w1qb_free = [(), ()]
    w2qb_free = [(), ()]
    h8_free = ()
    g_free = ()
    y1_free = [(), ()]
    y2_free = [(), ()]
    pex_free = [(), ()]
    den_free = ()
    rb_free = ()
    rec_free = [(), ()]
    sacc_free = [(), ()]
    x8r_free = ()
    att = {}

    def att_start(ac):
        st = {"t_exp": {}}
        st["w_qt"] = qload.issue(
            qtc[:].rearrange("p (dc t) -> p dc t", dc=DC),
            qT_s[ac].rearrange("(dc p) t -> p dc t", p=128),
            waits=list(qtc_free) + (p12_done if ac == 0 else []))
        att[ac] = st
        return st

    def att_quarter(ac, q4):
        st = att[ac]
        Pb = Pbuf_h[q4 % 2]
        for kt in range(QTL):
            kt2 = q4 * QTL + kt
            slot = kt2 % 2
            w_k = qload.issue(
                ktb[:, slot * DC * 128:(slot + 1) * DC * 128]
                .rearrange("p (dc f) -> p dc f", dc=DC),
                kT_s[kt2].rearrange("(dc p) f -> p dc f", p=128),
                waits=list(ktb_free[slot]) +
                (p12_done if ac == 0 and kt2 < 2 else []))
            bank = 4 + kt2 % 2

            def p3smm(te, slot=slot, bank=bank, wk=w_k, wq=st["w_qt"],
                      bf=bank_free[bank]):
                te.wait_ge(wk[0], wk[1])
                te.wait_ge(wq[0], wq[1])
                _w(te, bf)
                for dc in range(DC):
                    mm = te.matmul(pb[bank][:, 0:512],
                                   ktb[:, slot * DC * 128 + dc * 128:
                                       slot * DC * 128 + (dc + 1) * 128],
                                   qtc[:, dc * 512:(dc + 1) * 512],
                                   start=(dc == 0), stop=(dc == DC - 1))
                mm.then_inc(ST.h, 1)
            prog.on("tensor", p3smm)
            t_mm = ST.inc()
            ktb_free[slot] = [(ST.h, t_mm)]

            def p3exp(s, kt=kt, bank=bank, w=t_mm, Pb=Pb,
                      pf=pex_free[q4 % 2] if kt == 0 else ()):
                s.wait_ge(ST.h, w)
                _w(s, pf)
                _dr(s)
                s.activation(out=Pb[:, kt * 512:(kt + 1) * 512],
                             in_=pb[bank][:, 0:512], func=AF.Exp,
                             scale=1.0 / 32.0).then_inc(SA.h, 1)
                _dr(s)
            prog.on("scalar", p3exp)
            t_ex = SA.inc()
            bank_free[bank] = [(SA.h, t_ex)]

            def p3den(te, kt=kt, kt2=kt2, w=t_ex, Pb=Pb,
                      df=den_free if kt2 == 0 else ()):
                te.wait_ge(SA.h, w)
                _w(te, df)
                te.matmul(pb[6][0:1, 0:512], ones128[:],
                          Pb[:, kt * 512:(kt + 1) * 512],
                          start=(kt2 == 0), stop=(kt2 == NKT - 1)) \
                  .then_inc(ST.h, 1)
            prog.on("tensor", p3den)
            ST.inc()
        st["t_exp"][q4] = SA.v
        if q4 == NQ4 - 1:
            st["t_den"] = ST.v
            st["qdone"] = [(ST.h, ST.v)]

    def att_waves(ac, q4):
        st = att[ac]
        Pb = Pbuf_h[q4 % 2]
        p = ac % 2
        t_wave = ST.v
        for wave in range(2):
            for kt in range(QTL):
                kt2 = q4 * QTL + kt
                slot = kt % 2
                w_v = qload.issue(vtb[:, slot * 512:(slot + 1) * 512],
                                  v_s[kt2 * 128:(kt2 + 1) * 128,
                                      wave * 512:(wave + 1) * 512],
                                  waits=list(vtb_free[slot]) +
                                  (p12_done if ac == 0 and q4 == 0 and
                                   wave == 0 and kt < 2 else []))

                def p3wmm(te, kt=kt, slot=slot, wv=w_v, wex=st["t_exp"][q4],
                          Pb=Pb,
                          bfs=[list(bank_free[d]) for d in range(4)]
                          if kt == 0 else None):
                    te.wait_ge(wv[0], wv[1])
                    te.wait_ge(SA.h, wex)
                    if bfs is not None:
                        for bf in bfs:
                            _w(te, bf)
                    for dvt in range(4):
                        mm = te.matmul(pb[dvt][:, 0:512],
                                       vtb[:, slot * 512 + dvt * 128:
                                           slot * 512 + (dvt + 1) * 128],
                                       Pb[:, kt * 512:(kt + 1) * 512],
                                       start=(kt == 0), stop=(kt == QTL - 1))
                    mm.then_inc(ST.h, 1)
                prog.on("tensor", p3wmm)
                t_mm = ST.inc()
                vtb_free[slot] = [(ST.h, t_mm)]
            t_wave = ST.v
            for dvt in range(4):
                dvg = wave * 4 + dvt

                def p3sev(v, dvt=dvt, dvg=dvg, first=(q4 == 0), w=t_wave, p=p,
                          sf=sacc_free[p] if q4 == 0 and wave == 0 and
                          dvt == 0 else ()):
                    v.wait_ge(ST.h, w)
                    _w(v, sf)
                    _dr(v)
                    if first:
                        v.tensor_copy(out=s_acc[p][:, dvg * 512:(dvg + 1) * 512],
                                      in_=pb[dvt][:, 0:512]).then_inc(SV.h, 1)
                    else:
                        v.tensor_add(out=s_acc[p][:, dvg * 512:(dvg + 1) * 512],
                                     in0=s_acc[p][:, dvg * 512:(dvg + 1) * 512],
                                     in1=pb[dvt][:, 0:512]).then_inc(SV.h, 1)
                    _dr(v)
                prog.on("vector", p3sev)
                SV.inc()
            for dvt in range(4):
                bank_free[dvt] = [(SV.h, SV.v)]
        pex_free[q4 % 2] = [(ST.h, t_wave)]

    def att_rec(ac):
        nonlocal den_free
        st = att[ac]
        p = ac % 2

        def p3rec(v, w=st["t_den"], rf=rec_free[p], dn=den_p[p], rc=rcp_p[p]):
            v.wait_ge(ST.h, w)
            _w(v, rf)
            _dr(v)
            v.tensor_copy(out=dn, in_=pb[6][0:1, 0:512])
            _dr(v)
            v.reciprocal(out=rc, in_=dn).then_inc(SV.h, 1)
            _dr(v)
        prog.on("vector", p3rec)
        st["t_rc"] = SV.inc()
        den_free = [(SV.h, st["t_rc"])]

    def ffn1_group(c, q4, wx8):
        last_mm = 0
        for fi in range(NG):
            f1t = q4 * NG + fi
            slot = f1t % 2
            w_w1 = qload.issue(w1qb[:, slot * DC * 128:(slot + 1) * DC * 128],
                               w1q_s[f1t], waits=list(w1qb_free[slot]) +
                               (p12_done if c == 0 and f1t < 2 else []))
            bank = 4 + f1t % 2

            def p3f1(te, slot=slot, bank=bank, ww=w_w1, wx=wx8,
                     bf=bank_free[bank], hf=h8_free if f1t == 0 else ()):
                te.wait_ge(ww[0], ww[1])
                te.wait_ge(wx[0], wx[1])
                _w(te, bf)
                _w(te, hf)
                if USE_DR:
                    for kp in range(DC // 2):
                        mm = te.matmul(
                            pb[bank][:, 0:512],
                            w1qb[:, slot * DC * 128 + kp * 256:
                                 slot * DC * 128 + (kp + 1) * 256]
                            .rearrange("p (two f) -> p two f", two=2),
                            xn8c[:, kp * 1024:(kp + 1) * 1024]
                            .rearrange("p (two t) -> p two t", two=2),
                            start=(kp == 0), stop=(kp == DC // 2 - 1),
                            perf_mode=mybir.MatmulPerfMode.DoubleRow)
                else:
                    for kc in range(DC):
                        mm = te.matmul(
                            pb[bank][:, 0:512],
                            w1qb[:, slot * DC * 128 + kc * 128:
                                 slot * DC * 128 + (kc + 1) * 128],
                            xn8c[:, kc * 512:(kc + 1) * 512],
                            start=(kc == 0), stop=(kc == DC - 1))
                mm.then_inc(ST.h, 1)
            prog.on("tensor", p3f1)
            t_mm = ST.inc()
            last_mm = t_mm
            w1qb_free[slot] = [(ST.h, t_mm)]
            yslot = f1t % 2

            def p3f1e(s, f1t=f1t, bank=bank, yslot=yslot, w=t_mm, wb=w_b1q,
                      yf=y1_free[yslot]):
                s.wait_ge(ST.h, w)
                s.wait_ge(wb[0], wb[1])
                _w(s, yf)
                _dr(s)
                s.activation(out=y1st[:, yslot * 512:(yslot + 1) * 512],
                             in_=pb[bank][:, 0:512], func=AF.Relu,
                             bias=b1_t[:, f1t:f1t + 1]).then_inc(SA.h, 1)
                _dr(s)
            prog.on("scalar", p3f1e)
            t_ev = SA.inc()
            bank_free[bank] = [(SA.h, t_ev)]
            if f1t % 2 == 1:
                items = [{"in": y1st[:, j * 512:(j + 1) * 512],
                          "out": h8[:, (f1t - 1 + j) * 512:(f1t + j) * 512]}
                         for j in range(2)]
                outs = emit_quantize(bld, items, M=2, emin=-14, signed=False,
                                     pre=[(SA.h, t_ev)], band=WBAND,
                                     slack=SLACK_H, ulev=5)
                y1_free[0] = [outs[0]]
                y1_free[1] = [outs[1]]
        return last_mm

    def ffn2(c):
        nonlocal h8_free, rb_free
        t_h8 = (SV.h, SV.v)
        g_marks = []
        prev_ev = None
        for f2t in range(DC):
            slot = f2t % 2
            w_w2 = qload.issue(
                w2qb[:, slot * NF1T * 128:(slot + 1) * NF1T * 128]
                .rearrange("p (f j) -> p f j", f=NF1T),
                w2q_s[:, :, f2t * 128:(f2t + 1) * 128]
                .rearrange("f p j -> p f j"),
                waits=list(w2qb_free[slot]) +
                (p12_done if c == 0 and f2t < 2 else []))

            def p3f2(te, ww=w_w2, slot=slot, wh=t_h8, pe=prev_ev,
                     rbf=rb_free if f2t == 0 else ()):
                te.wait_ge(ww[0], ww[1])
                te.wait_ge(wh[0], wh[1])
                if pe is not None:
                    te.wait_ge(SA.h, pe)
                _w(te, rbf)
                if USE_DR:
                    for fp in range(NF1T // 2):
                        mm = te.matmul(
                            pb[7][:, 0:512],
                            w2qb[:, slot * NF1T * 128 + fp * 256:
                                 slot * NF1T * 128 + (fp + 1) * 256]
                            .rearrange("p (two f) -> p two f", two=2),
                            h8[:, fp * 1024:(fp + 1) * 1024]
                            .rearrange("p (two t) -> p two t", two=2),
                            start=(fp == 0), stop=(fp == NF1T // 2 - 1),
                            perf_mode=mybir.MatmulPerfMode.DoubleRow)
                else:
                    for fc in range(NF1T):
                        mm = te.matmul(
                            pb[7][:, 0:512],
                            w2qb[:, slot * NF1T * 128 + fc * 128:
                                 slot * NF1T * 128 + (fc + 1) * 128],
                            h8[:, fc * 512:(fc + 1) * 512],
                            start=(fc == 0), stop=(fc == NF1T - 1))
                mm.then_inc(ST.h, 1)
            prog.on("tensor", p3f2)
            t_mm = ST.inc()
            w2qb_free[slot] = [(ST.h, t_mm)]
            yslot = f2t % 2

            def p3f2e(s, f2t=f2t, yslot=yslot, w=t_mm, wb=w_b2q,
                      yf=y2_free[yslot]):
                s.wait_ge(ST.h, w)
                s.wait_ge(wb[0], wb[1])
                _w(s, yf)
                _dr(s)
                s.activation(out=y2st[:, yslot * 512:(yslot + 1) * 512],
                             in_=pb[7][:, 0:512], func=AF.Identity,
                             bias=b2_t[:, f2t:f2t + 1]).then_inc(SA.h, 1)
                _dr(s)
            prog.on("scalar", p3f2e)
            prev_ev = SA.inc()
            if f2t % 2 == 1:
                items = [{"in": y2st[:, j * 512:(j + 1) * 512],
                          "out": y2st[:, j * 512:(j + 1) * 512],
                          "f2": f2t - 1 + j} for j in range(2)]
                ws = emit_quantize(bld, items, M=7, emin=-20, signed=True,
                                   pre=[(SA.h, prev_ev)], slack=SLACK_Y2,
                                   ulev=6)
                for j, it in enumerate(items):
                    f2_ = it["f2"]
                    mslot = f2_ % 2
                    w_mk = qload.issue(
                        mkb[:, mslot * 512:(mslot + 1) * 512],
                        maskT[c].rearrange("(dc p) t -> p dc t", p=128)[:, f2_, :],
                        waits=mk_free[mslot])

                    def p3mk(v, it=it, mslot=mslot, w=ws[j], wm=w_mk):
                        v.wait_ge(w[0], w[1])
                        v.wait_ge(wm[0], wm[1])
                        _dr(v)
                        v.tensor_mul(out=it["out"], in0=it["out"],
                                     in1=mkb[:, mslot * 512:(mslot + 1) * 512]) \
                            .then_inc(SV.h, 1)
                        _dr(v)
                    prog.on("vector", p3mk)
                    t_m = SV.inc()
                    mk_free[mslot] = [(SV.h, t_m)]

                    def p3si(s, it=it, f2_=f2_, w=t_m, gf=g_free):
                        s.wait_ge(SV.h, w)
                        _w(s, gf)
                        _dr(s)
                        s.activation(out=gbuf[:, f2_ * 512:(f2_ + 1) * 512],
                                     in_=it["out"],
                                     func=AF.Silu).then_inc(SA.h, 1)
                        _dr(s)
                    prog.on("scalar", p3si)
                    g_marks.append(SA.inc())
                y2_free[0] = [(SA.h, g_marks[-2])]
                y2_free[1] = [(SA.h, g_marks[-1])]
        h8_free = [(ST.h, ST.v)]
        return g_marks[-1], prev_ev

    if PIPELINE:
        att_start(0)
        for q4 in range(NQ4):
            att_quarter(0, q4)
            if q4 > 0:
                att_waves(0, q4 - 1)
        att_waves(0, NQ4 - 1)
        att_rec(0)

    for c in range(NP3):
        cn = c + 1 if PIPELINE else c
        w_x8 = qload.issue(xn8c[:], xn8_s[c],
                           waits=list(x8r_free) + [xn8_spill[c]] +
                           (list(x8_free) if c == 0 else []))
        if c == 0:
            x8_free = ()
        w_xc = qload.issue(xcb[:].rearrange("p (dc t) -> p dc t", dc=DC),
                           xT[c].rearrange("(dc p) t -> p dc t", p=128),
                           waits=list(xc_free) + (list(xcb_free) if c == 0 else []))
        if c == 0:
            xcb_free = ()
        if cn < NP3:
            prev = cn - 1
            if prev in att:
                qtc_free = att[prev]["qdone"]
            att_start(cn)
        last_f1 = 0
        for q4 in range(NQ4):
            if cn < NP3:
                att_quarter(cn, q4)
                if q4 > 0:
                    att_waves(cn, q4 - 1)
            if PIPELINE:
                last_f1 = ffn1_group(c, q4, w_x8)
        if cn < NP3:
            att_waves(cn, NQ4 - 1)
            att_rec(cn)
        if not PIPELINE:
            for q4 in range(NQ4):
                last_f1 = ffn1_group(c, q4, w_x8)
        x8r_free = [(ST.h, last_f1)]
        t_gl, ev7 = ffn2(c)
        p = c % 2

        def p3rb(te, w=att[c]["t_rc"], e7=ev7, rb=rb_free, rc=rcp_p[p]):
            te.wait_ge(SV.h, w)
            te.wait_ge(SA.h, e7)
            _w(te, rb)
            te.matmul(pb[7][:, 0:512], ones1[:], rc, start=True, stop=True) \
              .then_inc(ST.h, 1)
        prog.on("tensor", p3rb)
        t_rb = ST.inc()
        rec_free[p] = [(ST.h, t_rb)]

        for f2t in range(DC):
            def p3cb(v, f2t=f2t, wg=t_gl, wrb=t_rb, wx=w_xc, p=p):
                v.wait_ge(SA.h, wg)
                v.wait_ge(ST.h, wrb)
                v.wait_ge(wx[0], wx[1])
                _dr(v)
                v.tensor_mul(out=s_acc[p][:, f2t * 512:(f2t + 1) * 512],
                             in0=s_acc[p][:, f2t * 512:(f2t + 1) * 512],
                             in1=pb[7][:, 0:512])
                _dr(v)
                v.tensor_mul(out=s_acc[p][:, f2t * 512:(f2t + 1) * 512],
                             in0=s_acc[p][:, f2t * 512:(f2t + 1) * 512],
                             in1=gbuf[:, f2t * 512:(f2t + 1) * 512])
                _dr(v)
                v.tensor_add(out=xcb[:, f2t * 512:(f2t + 1) * 512],
                             in0=xcb[:, f2t * 512:(f2t + 1) * 512],
                             in1=s_acc[p][:, f2t * 512:(f2t + 1) * 512]) \
                    .then_inc(SV.h, 1)
                _dr(v)
            prog.on("vector", p3cb)
            SV.inc()
        t_cb = SV.v
        rb_free = [(SV.h, t_cb)]
        g_free = [(SV.h, t_cb)]
        sacc_free[p] = [(SV.h, t_cb)]
        w_out = qstore.issue(outT[c].rearrange("(dc p) t -> p dc t", p=128),
                             xcb[:].rearrange("p (dc t) -> p dc t", dc=DC),
                             waits=[(SV.h, t_cb)])
        xc_free = [w_out]

    def fin(gq, waits=tuple(qstore.all_done() + qspill.all_done())):
        _w(gq, waits)
    prog.on("gpsimd", fin)

    with nc.Block() as block:
        @block.sync
        def _(s):
            for fn in prog.items["sync"]:
                fn(s)

        @block.scalar
        def _(s):
            for fn in prog.items["scalar"]:
                fn(s)

        @block.vector
        def _(v):
            for fn in prog.items["vector"]:
                fn(v)

        @block.gpsimd
        def _(g):
            for fn in prog.items["gpsimd"]:
                fn(g)

        @block.tensor
        def _(te):
            for fn in prog.items["tensor"]:
                fn(te)

    es.close()
    return nc



def _tf2x32(k0, k1, x0, x1):
    def rotl(x, d):
        return ((x << np.uint32(d)) | (x >> np.uint32(32 - d))).astype(np.uint32)
    ks0, ks1 = np.uint32(k0), np.uint32(k1)
    ks2 = np.uint32(ks0 ^ ks1 ^ np.uint32(0x1BD11BDA))
    x0 = (x0 + ks0).astype(np.uint32)
    x1 = (x1 + ks1).astype(np.uint32)
    r1, r2 = (13, 15, 26, 6), (17, 29, 16, 24)
    ks = [ks0, ks1, ks2]
    for i, rots in enumerate((r1, r2, r1, r2, r1)):
        for d in rots:
            x0 = (x0 + x1).astype(np.uint32)
            x1 = rotl(x1, d)
            x1 = (x0 ^ x1).astype(np.uint32)
        x0 = (x0 + ks[(i + 1) % 3]).astype(np.uint32)
        x1 = (x1 + ks[(i + 2) % 3] + np.uint32(i + 1)).astype(np.uint32)
    return x0, x1


def _bernoulli_keep(seed, p, shape):
    n = int(np.prod(shape))
    i = np.arange(n, dtype=np.uint64)
    c1 = (i >> np.uint64(32)).astype(np.uint32)
    c2 = (i & np.uint64(0xFFFFFFFF)).astype(np.uint32)
    b1, b2 = _tf2x32(0, seed, c1, c2)
    import os
    fp64_regime = os.environ.get("BASS_SIM_FP64") == "1"
    try:
        import jax
        fp64_regime = fp64_regime or bool(jax.config.read("jax_enable_x64"))
    except Exception:
        pass
    if fp64_regime:
        bits = (b1.astype(np.uint64) << np.uint64(32)) | b2.astype(np.uint64)
        fb = ((bits >> np.uint64(12)) | np.uint64(0x3FF0000000000000)) \
            .view(np.float64) - 1.0
        return (fb < np.float64(p)).reshape(shape)
    bits = b1 ^ b2
    fb = ((bits >> np.uint32(9)) | np.uint32(0x3F800000)).view(np.float32) \
        - np.float32(1.0)
    return (fb < np.float32(p)).reshape(shape)


def _chunked_T(a):
    T, D = a.shape
    aT = np.ascontiguousarray(a.T)
    return np.ascontiguousarray(aT.reshape(D, T // 512, 512).transpose(1, 0, 2))


def kernel(L0_attn_t, L0_attn_W_qkv, L0_attn_b_qkv, L0_ffn_W1, L0_ffn_b1,
           L0_ffn_W2, L0_ffn_b2):
    x = np.asarray(L0_attn_t)
    B, T, D = x.shape
    F1 = np.asarray(L0_ffn_W1).shape[0]
    cfg = Cfg(T=T, F1=F1)
    nc = build_nc(cfg)

    keep = _bernoulli_keep(42, 0.8, (B, T, D))
    maskscale = (keep.astype(np.float32) * np.float32(1.25)).astype(
        ml_dtypes.bfloat16)

    wqkvT = np.ascontiguousarray(np.asarray(L0_attn_W_qkv).T)
    b_qkv = np.asarray(L0_attn_b_qkv)
    bqk = np.ascontiguousarray(b_qkv[:2 * D].reshape(16, 128).T)
    bvrow = np.ascontiguousarray(b_qkv[2 * D:][None, :])
    w1T = np.ascontiguousarray(np.asarray(L0_ffn_W1).T)
    w2T = np.ascontiguousarray(np.asarray(L0_ffn_W2).T)
    b1r = np.ascontiguousarray(np.asarray(L0_ffn_b1).reshape(-1, 128).T)
    b2r = np.ascontiguousarray(np.asarray(L0_ffn_b2).reshape(-1, 128).T)

    in_maps = []
    for b in range(B):
        in_maps.append({
            "xT": _chunked_T(x[b]), "wqkvT": wqkvT, "bqk": bqk, "bvrow": bvrow,
            "w1T": w1T, "w2T": w2T, "b1r": b1r, "b2r": b2r,
            "maskT": _chunked_T(maskscale[b]),
        })
    res = run_bass_kernel_spmd(nc, in_maps, core_ids=list(range(B)))
    out = np.empty((B, T, D), dtype=np.float32)
    for b in range(B):
        o = np.asarray(res.results[b]["outT"])
        out[b] = o.transpose(1, 0, 2).reshape(D, T).T
    return out


# version 10
# speedup vs baseline: 1.0322x; 1.0322x over previous
import os
import numpy as np
import ml_dtypes
import concourse.bass as bass
import concourse.mybir as mybir
from concourse.bass_utils import run_bass_kernel_spmd

SIM_DRAIN = (os.environ.get("BASS_SIM_FP64") == "1"
             or os.environ.get("KERNEL_FORCE_DRAIN") == "1")


_DR_ENG = os.environ.get("KERNEL_DRAIN_ENG", "")
_DR_TAGS = set(os.environ.get("KERNEL_DRAIN_TAGS", "").split(","))


def _dr(e, tag="aux"):
    if SIM_DRAIN or (_DR_ENG and _DR_ENG in type(e).__name__)             or tag in _DR_TAGS:
        e.drain()

F32 = mybir.dt.float32
BF16 = mybir.dt.bfloat16
FP8 = mybir.dt.float8e5
AF = mybir.ActivationFunctionType
OP = mybir.AluOpType

LN2 = float(np.log(2.0))
LN2_HI = float(np.float32(np.log(2.0)))
NEG_D = float(LN2 - LN2_HI)
INV_LN2 = float(1.0 / np.log(2.0))
DELTA = 3e-9
WBAND = 2.0 ** -6
SLACK_H = 2.0 ** -22 + 2.0 ** -35
SLACK_Y2 = 2.0 ** -17 + 2.0 ** -29
USE_DR = os.environ.get("KERNEL_NO_DR") != "1"
PIPELINE = os.environ.get("KERNEL_NO_PIPE") != "1"


class Cfg:
    def __init__(self, T=4096, F1=4096, p3_chunks=None):
        self.D = 1024
        self.T = T
        self.F1 = F1
        self.NC = T // 512
        self.NKT = T // 128
        self.NF1T = F1 // 128
        self.DC = 8
        self.p3_chunks = self.NC if p3_chunks is None else p3_chunks


class Sem:
    def __init__(self, es, nc, name):
        self.h = es.enter_context(nc.semaphore(name))
        self.v = 0

    def inc(self, n=1):
        self.v += n
        return self.v


class DQ:

    def __init__(self, es, nc, prog, stream, name, k=4):
        self.sems = [Sem(es, nc, f"{name}{i}") for i in range(k)]
        self.k = k
        self.n = 0
        self.prog = prog
        self.stream = stream

    def issue(self, out, in_, waits=()):
        i = self.n % self.k
        self.n += 1
        s = self.sems[i]
        pre = s.v
        tgt = s.inc(16)

        def fn(e, out=out, in_=in_, waits=tuple(waits), s=s, pre=pre):
            for (h, v) in waits:
                e.wait_ge(h, v)
            if pre:
                e.wait_ge(s.h, pre)
            e.dma_start(out=out, in_=in_).then_inc(s.h, 16)
        self.prog.on(self.stream, fn)
        return (s.h, tgt)

    def all_done(self):
        return [(s.h, s.v) for s in self.sems if s.v]


class Prog:
    ENGS = ("sync", "scalar", "vector", "gpsimd", "tensor")

    def __init__(self):
        self.items = {e: [] for e in self.ENGS}

    def on(self, eng, fn):
        self.items[eng].append(fn)


def _w(e, waits):
    for (h, v) in waits:
        e.wait_ge(h, v)



def emit_quantize(bld, items, M, emin, signed, pre=(), band=0.0, slack=DELTA,
                  ulev=5):
    prog = bld["prog"]
    SA, SV = bld["SA"], bld["SV"]
    tmp = bld["tmp"]
    hist = bld["tmp_hist"]
    CST = bld["CST"]
    pre = tuple(pre) + (bld["w_cst"],)
    outs = []
    for i0 in range(0, len(items), 2):
        pair = items[i0:i0 + 2]
        for j, it in enumerate(pair):
            x = it["in"]
            fd = x.shape[-1]
            T_, A, SG = tmp["T"][j][:, :fd], tmp["A"][j][:, :fd], tmp["SG"][j][:, :fd]
            hw = hist[j]

            def a1(s, x=x, T_=T_, A=A, SG=SG, hw=hw, pre=tuple(pre)):
                _w(s, pre)
                if hw:
                    _w(s, [hw])
                _dr(s, "chs")
                if signed:
                    s.activation(out=T_, in_=x, func=AF.Abs)
                    _dr(s, "chs")
                    s.activation(out=SG, in_=x, func=AF.Sign)
                    _dr(s, "chs")
                    s.activation(out=A, in_=T_, func=AF.Ln, bias=CST[1e-30])
                else:
                    s.activation(out=A, in_=x, func=AF.Ln, bias=CST[1e-30])
                _dr(s, "chs")
                s.activation(out=A, in_=A, func=AF.Identity, scale=INV_LN2) \
                    .then_inc(SA.h, 1)
                _dr(s, "chs")
            prog.on("scalar", a1)
            it["a1"] = SA.inc()
        for j, it in enumerate(pair):
            fd = it["in"].shape[-1]
            A, B, G = (tmp["A"][j][:, :fd], tmp["B"][j][:, :fd],
                       tmp["G"][j][:, :fd])

            def v1(v, w=it["a1"], A=A, B=B, G=G):
                v.wait_ge(SA.h, w)
                _dr(v, "chv")
                v.tensor_scalar(out=B, in0=A, scalar1=float(emin),
                                scalar2=float(-emin), op0=OP.max, op1=OP.add)
                _dr(v, "chv")
                src = B
                for lv in (2 ** p for p in range(ulev - 1, -1, -1)):
                    v.tensor_scalar(out=G, in0=src, scalar1=float(lv),
                                    scalar2=float(-lv), op0=OP.is_ge,
                                    op1=OP.mult)
                    _dr(v, "chv")
                    v.tensor_add(out=A, in0=src, in1=G)
                    _dr(v, "chv")
                    src = A
                v.scalar_tensor_tensor(out=B, in0=B, scalar=float(emin - M),
                                       in1=A, op0=OP.add, op1=OP.subtract)
                _dr(v, "chv")
                v.tensor_scalar(out=A, in0=B, scalar1=LN2_HI, scalar2=None,
                                op0=OP.mult)
                _dr(v, "chv")
                v.scalar_tensor_tensor(out=A, in0=B, scalar=NEG_D, in1=A,
                                       op0=OP.mult, op1=OP.add) \
                    .then_inc(SV.h, 1)
                _dr(v, "chv")
            prog.on("vector", v1)
            it["v1"] = SV.inc()
        for j, it in enumerate(pair):
            fd = it["in"].shape[-1]
            A = tmp["A"][j][:, :fd]
            ISB, SRB = tmp["ISB"][j][:, :fd], tmp["SRB"][j][:, :fd]

            def a2(s, w=it["v1"], A=A, ISB=ISB, SRB=SRB):
                s.wait_ge(SV.h, w)
                _dr(s, "chs")
                s.activation(out=ISB, in_=A, func=AF.Exp, scale=-1.0)
                _dr(s, "chs")
                s.activation(out=SRB, in_=A, func=AF.Exp) \
                    .then_inc(SA.h, 1)
                _dr(s, "chs")
            prog.on("scalar", a2)
            it["a2"] = SA.inc()
        for j, it in enumerate(pair):
            x = it["in"]
            fd = x.shape[-1]
            T_, A, B, G, SG = (tmp["T"][j][:, :fd], tmp["A"][j][:, :fd],
                               tmp["B"][j][:, :fd], tmp["G"][j][:, :fd],
                               tmp["SG"][j][:, :fd])
            ISB, SRB = tmp["ISB"][j][:, :fd], tmp["SRB"][j][:, :fd]

            def v2(v, w=it["a2"], out=it["out"], x=x, T_=T_, A=A, B=B, G=G,
                   SG=SG, ISB=ISB, SRB=SRB):
                v.wait_ge(SA.h, w)
                _dr(v, "chv")
                v.tensor_mul(out=A, in0=(T_ if signed else x), in1=ISB)
                _dr(v, "chv")
                src = A
                for lv in (2 ** p for p in range(M, 0, -1)):
                    v.tensor_scalar(out=G, in0=src, scalar1=float(lv),
                                    scalar2=float(-lv), op0=OP.is_ge,
                                    op1=OP.mult)
                    _dr(v, "chv")
                    v.tensor_add(out=B, in0=src, in1=G)
                    _dr(v, "chv")
                    src = B
                v.tensor_scalar(out=G, in0=B, scalar1=float(0.5 + band),
                                scalar2=slack, op0=OP.subtract,
                                op1=OP.is_gt)
                _dr(v, "chv")
                v.tensor_scalar(out=T_, in0=B, scalar1=float(1.5 - band),
                                scalar2=-slack, op0=OP.subtract,
                                op1=OP.is_ge)
                _dr(v, "chv")
                v.tensor_add(out=G, in0=G, in1=T_)
                _dr(v, "chv")
                v.tensor_sub(out=A, in0=A, in1=B)
                _dr(v, "chv")
                v.tensor_add(out=A, in0=A, in1=G)
                _dr(v, "chv")
                if signed:
                    v.tensor_mul(out=A, in0=A, in1=SRB)
                    _dr(v, "chv")
                    v.tensor_mul(out=out, in0=A, in1=SG).then_inc(SV.h, 1)
                else:
                    v.tensor_mul(out=out, in0=A, in1=SRB).then_inc(SV.h, 1)
                _dr(v, "chv")
            prog.on("vector", v2)
            it["v2"] = SV.inc()
            hist[j] = (SV.h, it["v2"])
            outs.append((SV.h, it["v2"]))
    return outs



def build_nc(cfg: Cfg):
    import contextlib
    nc = bass.Bass("TRN2", target_bir_lowering=False, debug=False)
    es = contextlib.ExitStack()
    D, T, F1 = cfg.D, cfg.T, cfg.F1
    NCH, NKT, NF1T, DC = cfg.NC, cfg.NKT, cfg.NF1T, cfg.DC
    NQK = 16

    xT = nc.dram_tensor("xT", [NCH, D, 512], F32, kind="ExternalInput").ap()
    wqkvT = nc.dram_tensor("wqkvT", [D, 3 * D], F32, kind="ExternalInput").ap()
    bqk = nc.dram_tensor("bqk", [128, NQK], F32, kind="ExternalInput").ap()
    bvrow = nc.dram_tensor("bvrow", [1, D], F32, kind="ExternalInput").ap()
    w1T = nc.dram_tensor("w1T", [D, F1], F32, kind="ExternalInput").ap()
    w2T = nc.dram_tensor("w2T", [F1, D], F32, kind="ExternalInput").ap()
    b1r = nc.dram_tensor("b1r", [128, NF1T], F32, kind="ExternalInput").ap()
    b2r = nc.dram_tensor("b2r", [128, DC], F32, kind="ExternalInput").ap()
    maskT = nc.dram_tensor("maskT", [NCH, D, 512], BF16, kind="ExternalInput").ap()
    outT = nc.dram_tensor("outT", [NCH, D, 512], F32, kind="ExternalOutput").ap()
    qT_s = nc.dram_tensor("qT_s", [NCH, D, 512], F32, kind="Internal").ap()
    kT_s = nc.dram_tensor("kT_s", [NKT, D, 128], F32, kind="Internal").ap()
    v_s = nc.dram_tensor("v_s", [T, D], F32, kind="Internal").ap()
    xn8_s = nc.dram_tensor("xn8_s", [NCH, 128, DC * 512], FP8, kind="Internal").ap()
    w1q_s = nc.dram_tensor("w1q_s", [NF1T, 128, DC * 128], FP8, kind="Internal").ap()
    w2q_s = nc.dram_tensor("w2q_s", [NF1T, 128, D], FP8, kind="Internal").ap()

    prog = Prog()
    ST = Sem(es, nc, "st_t")
    SV = Sem(es, nc, "st_v")
    SA = Sem(es, nc, "st_a")
    qload = DQ(es, nc, prog, "sync", "ql", k=6)
    qspill = DQ(es, nc, prog, "gpsimd", "qs", k=4)
    qstore = DQ(es, nc, prog, "gpsimd", "qo", k=2)

    sbuf = lambda name, shape, dt: es.enter_context(nc.sbuf_tensor(name, shape, dt))

    NQ4 = 4
    QTL = NKT // NQ4
    PBW = max(QTL * 512, DC * 512)
    CARVE = (PBW + 2 * DC * 512 + DC * 512 + DC * 512 + DC * 512 +
             2 * 512 + 2 * 512 + 2 * 512 + 2 * DC * 128)
    arf = sbuf("arf", [128, CARVE], F32)
    off = 0

    def carve(n):
        nonlocal off
        a = arf[:, off:off + n]
        off += n
        return a
    Pbuf = carve(PBW)
    sstg = carve(DC * 512)
    s_acc2 = carve(2 * DC * 512)
    s_acc = [s_acc2[:, 0:DC * 512], s_acc2[:, DC * 512:2 * DC * 512]]
    qtc = carve(DC * 512)
    gbuf = carve(DC * 512)
    y1st = carve(2 * 512)
    y2st = carve(2 * 512)
    vtb = carve(2 * 512)
    ktb = carve(2 * DC * 128)
    assert off == CARVE
    xnc = Pbuf[:, 0:DC * 512]
    sqb = s_acc[0][:, 0:2 * 512]
    stg = s_acc[0][:, 2 * 512:4 * 512]
    wqb = qtc[:, 0:2 * DC * 128]
    wvb = gbuf[:, 0:2 * 512]

    ones128 = sbuf("ones128", [128, 1], F32)
    ones1 = sbuf("ones1", [1, 128], F32)
    bqk_t = sbuf("bqk_t", [128, NQK], F32)
    b1_t = sbuf("b1_t", [128, NF1T], F32)
    b2_t = sbuf("b2_t", [128, DC], F32)
    rows = sbuf("rows", [1, 2 * 1024], F32)
    den_p = [rows[0:1, 0:512], rows[0:1, 1024:1536]]
    rcp_p = [rows[0:1, 512:1024], rows[0:1, 1536:2048]]
    bv_t = sbuf("bv_t", [1, D], F32)
    xcb = sbuf("xcb", [128, DC * 512], F32)
    xn8c = sbuf("xn8c", [128, DC * 512], FP8)
    w1qb = sbuf("w1qb", [128, 2 * DC * 128], FP8)
    w2qb = sbuf("w2qb", [128, 2 * NF1T * 128], FP8)
    h8 = sbuf("h8", [128, NF1T * 512], FP8)
    mkb = sbuf("mkb", [128, 2 * 512], BF16)

    QF = 512
    tmp = {}
    _gdt = F32
    for nm in ("T", "A", "B"):
        th = sbuf("tmq_" + nm, [128, 2 * QF], F32)
        tmp[nm] = [th[:, 0:QF], th[:, QF:2 * QF]]
    for nm in ("G", "SG"):
        th = sbuf("tmq_" + nm, [128, 2 * QF], _gdt)
        tmp[nm] = [th[:, 0:QF], th[:, QF:2 * QF]]
    for nm in ("ISB", "SRB"):
        th = sbuf("tmq_" + nm, [128, 2 * QF], BF16)
        tmp[nm] = [th[:, 0:QF], th[:, QF:2 * QF]]

    pb = [es.enter_context(nc.psum_tensor("pb%d" % i, [128, 512], F32))
          for i in range(8)]
    bank_free = [()] * 8

    _cvals = [1e-30, 1e-5]
    cst = sbuf("cst", [128, len(_cvals)], F32)
    cst_d = nc.inline_tensor(
        np.tile(np.asarray(_cvals, np.float32), (128, 1)), name="cstd")
    CST = {v: cst[:, i:i + 1] for i, v in enumerate(_cvals)}
    CST1 = {v: cst[0:1, i:i + 1] for i, v in enumerate(_cvals)}

    bld = {"nc": nc, "prog": prog, "SA": SA, "SV": SV, "tmp": tmp,
           "tmp_hist": [None, None], "CST": CST}

    def init_v(v):
        v.memset(ones128[:], 1.0)
        _dr(v)
        v.memset(ones1[:], 1.0).then_inc(SV.h, 1)
        _dr(v)
    prog.on("vector", init_v)
    t_ones = (SV.h, SV.inc())

    w_cst = qload.issue(cst[:], cst_d.ap())
    bld["w_cst"] = w_cst
    w_bqk = qload.issue(bqk_t[:], bqk[:])
    w_b1 = qload.issue(b1_t[:], b1r[:])
    w_b2 = qload.issue(b2_t[:], b2r[:])
    w_bv = qload.issue(bv_t[:], bvrow[:])
    w_b1q = emit_quantize(bld, [{"in": b1_t[:], "out": b1_t[:]}], M=2, emin=-14,
                          signed=True, pre=[w_b1])[-1]
    w_b2q = emit_quantize(bld, [{"in": b2_t[:], "out": b2_t[:]}], M=2, emin=-14,
                          signed=True, pre=[w_b2])[-1]

    wq_pieces = [("w1", i) for i in range(NF1T)] + [("w2", i) for i in range(NF1T)]
    wq_state = {"i": 0, "land_free": [(), ()]}

    def emit_wq_piece():
        if wq_state["i"] >= len(wq_pieces):
            return
        kind, a = wq_pieces[wq_state["i"]]
        slot = wq_state["i"] % 2
        wq_state["i"] += 1
        land = xcb[:, slot * DC * 128:(slot + 1) * DC * 128]
        lw = list(wq_state["land_free"][slot]) + list(wq_state.get("xcb_guard", ()))
        if kind == "w1":
            src = w1T[:, a * 128:(a + 1) * 128].rearrange("(kc p) f -> p kc f", p=128)
            dst_l = land.rearrange("p (kc f) -> p kc f", kc=DC)
            w_ld = qload.issue(dst_l, src, waits=lw)
            dst = w1qb[:, slot * DC * 128:(slot + 1) * DC * 128]
        else:
            src = w2T[a * 128:(a + 1) * 128, 0:D]
            w_ld = qload.issue(land, src, waits=lw)
            dst = w2qb[:, slot * NF1T * 128: slot * NF1T * 128 + D]
        items = []
        for half in range(2):
            items.append({"in": land[:, half * 512:(half + 1) * 512],
                          "out": dst[:, half * 512:(half + 1) * 512]})
        pre = [w_ld] + list(wq_state.setdefault("dst_free", [(), ()])[slot])
        outs = emit_quantize(bld, items, M=2, emin=-14, signed=True, pre=pre,
                             ulev=4)
        wq_state["land_free"][slot] = [(SA.h, SA.v)]
        if kind == "w1":
            sp = qspill.issue(w1q_s[a], dst, waits=[outs[-1]])
        else:
            sp = qspill.issue(w2q_s[a], dst, waits=[outs[-1]])
        wq_state.setdefault("dst_free", [(), ()])[slot] = [sp]
        return

    xcb_free, xnc_free, x8_free = (), (), ()
    sq_free = [(), ()]
    stg_free = [(), ()]
    wqb_free = [(), ()]
    wvb_free = [(), ()]
    xn8_spill = [None] * NCH
    for c in range(NCH):
        if c > 0:
            xcb_free = list(xcb_free) + [(SA.h, SA.v)]
        w_x = qload.issue(xcb[:].rearrange("p (dc t) -> p dc t", dc=DC),
                          xT[c].rearrange("(dc p) t -> p dc t", p=128),
                          waits=xcb_free)
        for dc in range(DC):
            slot = dc % 2

            def p1sq(s, dc=dc, slot=slot, w=w_x, bf=sq_free[slot]):
                s.wait_ge(w[0], w[1])
                _w(s, bf)
                _dr(s)
                s.activation(out=sqb[:, slot * 512:(slot + 1) * 512],
                             in_=xcb[:, dc * 512:(dc + 1) * 512],
                             func=AF.Square).then_inc(SA.h, 1)
                _dr(s)
            prog.on("scalar", p1sq)
            t_sq = SA.inc()

            def p1mm(te, dc=dc, slot=slot, wx=w_x, wsq=t_sq,
                     bf0=bank_free[0], bf1=bank_free[1], wo=t_ones):
                te.wait_ge(wx[0], wx[1])
                te.wait_ge(SA.h, wsq)
                if dc == 0:
                    te.wait_ge(wo[0], wo[1])
                    _w(te, bf0)
                    _w(te, bf1)
                te.matmul(pb[0][0:1, 0:512], ones128[:],
                          xcb[:, dc * 512:(dc + 1) * 512],
                          start=(dc == 0), stop=(dc == DC - 1))
                te.matmul(pb[1][0:1, 0:512], ones128[:],
                          sqb[:, slot * 512:(slot + 1) * 512],
                          start=(dc == 0), stop=(dc == DC - 1)).then_inc(ST.h, 1)
            prog.on("tensor", p1mm)
            t_mm = ST.inc()
            sq_free[slot] = [(ST.h, t_mm)]
        t_stats = ST.v
        mean = rows[0:1, 0:512]
        var = rows[0:1, 512:1024]
        rstd = rows[0:1, 1024:1536]

        def p1r1(v, w=t_stats):
            v.wait_ge(ST.h, w)
            _dr(v)
            v.tensor_single_scalar(out=mean, in_=pb[0][0:1, 0:512],
                                   scalar=1.0 / D, op=OP.mult)
            _dr(v)
            v.tensor_single_scalar(out=var, in_=pb[1][0:1, 0:512],
                                   scalar=1.0 / D, op=OP.mult)
            _dr(v)
            v.tensor_mul(out=rstd, in0=mean, in1=mean)
            _dr(v)
            v.tensor_sub(out=var, in0=var, in1=rstd).then_inc(SV.h, 1)
            _dr(v)
        prog.on("vector", p1r1)
        t_r1 = SV.inc()
        bank_free[0] = [(SV.h, t_r1)]
        bank_free[1] = [(SV.h, t_r1)]

        def p1r2(s, w=t_r1, wc=w_cst):
            s.wait_ge(SV.h, w)
            s.wait_ge(wc[0], wc[1])
            _dr(s)
            s.activation(out=var, in_=var, func=AF.Sqrt,
                         bias=CST1[1e-5]).then_inc(SA.h, 1)
            _dr(s)
        prog.on("scalar", p1r2)
        t_r2 = SA.inc()

        def p1r3(v, w=t_r2):
            v.wait_ge(SA.h, w)
            _dr(v)
            v.reciprocal(out=rstd, in_=var)
            _dr(v)
            v.tensor_mul(out=mean, in0=mean,
                         in1=rstd).then_inc(SV.h, 1)
            _dr(v)
        prog.on("vector", p1r3)
        t_r3 = SV.inc()

        def p1bc(te, w=t_r3, bf2=bank_free[2], bf3=bank_free[3]):
            te.wait_ge(SV.h, w)
            _w(te, bf2)
            _w(te, bf3)
            te.matmul(pb[2][:, 0:512], ones1[:], rstd, start=True, stop=True)
            te.matmul(pb[3][:, 0:512], ones1[:], mean, start=True, stop=True) \
              .then_inc(ST.h, 1)
        prog.on("tensor", p1bc)
        t_bc = ST.inc()
        for dc in range(DC):
            def p1xn(v, dc=dc, w=t_bc, xf=xnc_free if dc == 0 else ()):
                v.wait_ge(ST.h, w)
                _w(v, xf)
                _dr(v)
                v.tensor_mul(out=xnc[:, dc * 512:(dc + 1) * 512],
                             in0=xcb[:, dc * 512:(dc + 1) * 512],
                             in1=pb[2][:, 0:512])
                _dr(v)
                v.tensor_sub(out=xnc[:, dc * 512:(dc + 1) * 512],
                             in0=xnc[:, dc * 512:(dc + 1) * 512],
                             in1=pb[3][:, 0:512]).then_inc(SV.h, 1)
                _dr(v)
            prog.on("vector", p1xn)
            SV.inc()
        t_xn = SV.v
        bank_free[2] = [(SV.h, t_xn)]
        bank_free[3] = [(SV.h, t_xn)]
        xcb_free = [(SV.h, t_xn), (SA.h, SA.v), (ST.h, t_stats)]
        wq_state["xcb_guard"] = list(xcb_free)
        items = [{"in": xnc[:, dc * 512:(dc + 1) * 512],
                  "out": xn8c[:, dc * 512:(dc + 1) * 512]} for dc in range(DC)]
        pre = [(SV.h, t_xn)] + list(x8_free)
        q_outs = emit_quantize(bld, items, M=2, emin=-14, signed=True, pre=pre)
        xn8_spill[c] = qspill.issue(xn8_s[c], xn8c[:], waits=[q_outs[-1]])
        x8_free = [xn8_spill[c]]

        for ft in range(NQK):
            slot = ft % 2
            w_w = qload.issue(wqb[:, slot * DC * 128:(slot + 1) * DC * 128]
                              .rearrange("p (kc f) -> p kc f", kc=DC),
                              wqkvT[:, ft * 128:(ft + 1) * 128]
                              .rearrange("(kc p) f -> p kc f", p=128),
                              waits=wqb_free[slot])
            bank = 4 + ft % 2

            def p2mm(te, ft=ft, slot=slot, bank=bank, ww=w_w, wxn=t_xn,
                     bf=bank_free[bank]):
                te.wait_ge(ww[0], ww[1])
                te.wait_ge(SV.h, wxn)
                _w(te, bf)
                for dc in range(DC):
                    mm = te.matmul(pb[bank][:, 0:512],
                                   wqb[:, slot * DC * 128 + dc * 128:
                                       slot * DC * 128 + (dc + 1) * 128],
                                   xnc[:, dc * 512:(dc + 1) * 512],
                                   start=(dc == 0), stop=(dc == DC - 1))
                mm.then_inc(ST.h, 1)
            prog.on("tensor", p2mm)
            t_mm = ST.inc()
            wqb_free[slot] = [(ST.h, t_mm)]
            sslot = ft % 2

            def p2ev(s, ft=ft, bank=bank, sslot=sslot, w=t_mm, w0=w_bqk,
                     sf=stg_free[sslot]):
                s.wait_ge(ST.h, w)
                s.wait_ge(w0[0], w0[1])
                _w(s, sf)
                _dr(s)
                s.activation(out=stg[:, sslot * 512:(sslot + 1) * 512],
                             in_=pb[bank][:, 0:512], func=AF.Identity,
                             bias=bqk_t[:, ft:ft + 1]).then_inc(SA.h, 1)
                _dr(s)
            prog.on("scalar", p2ev)
            t_ev = SA.inc()
            bank_free[bank] = [(SA.h, t_ev)]
            if ft < 8:
                dst = qT_s[c][ft * 128:(ft + 1) * 128, :]
                src = stg[:, sslot * 512:(sslot + 1) * 512]
            else:
                f2 = ft - 8
                dst = kT_s[c * 4:(c + 1) * 4, f2 * 128:(f2 + 1) * 128, :] \
                    .rearrange("j d f -> d j f")
                src = stg[:, sslot * 512:(sslot + 1) * 512] \
                    .rearrange("p (j f) -> p j f", j=4)
            sp = qspill.issue(dst, src, waits=[(SA.h, t_ev)])
            stg_free[sslot] = [sp]
            if ft % 2 == 1:
                emit_wq_piece()
        for dvc in range(2):
            for dc in range(DC):
                slot = dc % 2
                w_wv = qload.issue(
                    wvb[:, slot * 512:(slot + 1) * 512],
                    wqkvT[dc * 128:(dc + 1) * 128, 2 * D + dvc * 512:
                          2 * D + (dvc + 1) * 512],
                    waits=wvb_free[slot])

                def p2vmm(te, dvc=dvc, dc=dc, slot=slot, ww=w_wv, wxn=t_xn,
                          wb=w_bv,
                          bfs=[list(bank_free[4 + b]) for b in range(4)]
                          if dc == 0 else None):
                    te.wait_ge(ww[0], ww[1])
                    if dc == 0:
                        te.wait_ge(SV.h, wxn)
                        te.wait_ge(wb[0], wb[1])
                        for bf in bfs:
                            _w(te, bf)
                    for tt in range(4):
                        mm = te.matmul(pb[4 + tt][:, 0:512],
                                       xnc[:, dc * 512 + tt * 128:
                                           dc * 512 + (tt + 1) * 128],
                                       wvb[:, slot * 512:(slot + 1) * 512],
                                       start=(dc == 0), stop=False)
                    if dc == DC - 1:
                        for tt in range(4):
                            mm = te.matmul(pb[4 + tt][:, 0:512], ones1[:],
                                           bv_t[0:1, dvc * 512:(dvc + 1) * 512],
                                           start=False, stop=True)
                    mm.then_inc(ST.h, 1)
                prog.on("tensor", p2vmm)
                t_mmv = ST.inc()
                wvb_free[slot] = [(ST.h, t_mmv)]
            t_vdone = ST.v
            for tt in range(4):
                sslot = tt % 2

                def p2vev(v, tt=tt, sslot=sslot, w=t_vdone, sf=stg_free[sslot]):
                    v.wait_ge(ST.h, w)
                    _w(v, sf)
                    _dr(v)
                    v.tensor_copy(out=stg[:, sslot * 512:(sslot + 1) * 512],
                                  in_=pb[4 + tt][:, 0:512]).then_inc(SV.h, 1)
                    _dr(v)
                prog.on("vector", p2vev)
                t_ev = SV.inc()
                bank_free[4 + tt] = [(SV.h, t_ev)]
                sp = qspill.issue(
                    v_s[(c * 4 + tt) * 128:(c * 4 + tt + 1) * 128,
                        dvc * 512:(dvc + 1) * 512],
                    stg[:, sslot * 512:(sslot + 1) * 512], waits=[(SV.h, t_ev)])
                stg_free[sslot] = [sp]
        xnc_free = [(ST.h, ST.v), (SV.h, SV.v), (SA.h, SA.v)]
        emit_wq_piece()

    while wq_state["i"] < len(wq_pieces):
        emit_wq_piece()
    t_wq_done = (SV.h, SV.v)
    p12_done = qspill.all_done()
    xcb_free = list(xcb_free)
    xnc_free = list(xnc_free) + [(SA.h, SA.v)]

    NP3 = cfg.p3_chunks
    NG = NF1T // NQ4
    qtc_free, xc_free = (), ()
    mk_free = [(), ()]
    ktb_free = [(), ()]
    vtb_free = [(), ()]
    w1qb_free = [(), ()]
    w2qb_free = [(), ()]
    h8_free = ()
    g_free = ()
    y1_free = [(), ()]
    y2_free = [(), ()]
    pexp_free = ()
    sstg_free = [()] * 8
    den_free = ()
    rb_free = ()
    rec_free = [(), ()]
    sacc_free = [(), ()]
    x8r_free = ()
    att = {}

    def att_start(ac):
        st = {"t_exp": {}}
        st["w_qt"] = qload.issue(
            qtc[:].rearrange("p (dc t) -> p dc t", dc=DC),
            qT_s[ac].rearrange("(dc p) t -> p dc t", p=128),
            waits=list(qtc_free) + (p12_done if ac == 0 else []))
        att[ac] = st
        return st

    def att_quarter(ac, q4):
        st = att[ac]
        for kt in range(QTL):
            kt2 = q4 * QTL + kt
            slot = kt2 % 2
            w_k = qload.issue(
                ktb[:, slot * DC * 128:(slot + 1) * DC * 128]
                .rearrange("p (dc f) -> p dc f", dc=DC),
                kT_s[kt2].rearrange("(dc p) f -> p dc f", p=128),
                waits=list(ktb_free[slot]) +
                (p12_done if ac == 0 and kt2 < 2 else []))
            bank = 4 + kt2 % 2

            def p3smm(te, slot=slot, bank=bank, wk=w_k, wq=st["w_qt"],
                      bf=bank_free[bank]):
                te.wait_ge(wk[0], wk[1])
                te.wait_ge(wq[0], wq[1])
                _w(te, bf)
                for dc in range(DC):
                    mm = te.matmul(pb[bank][:, 0:512],
                                   ktb[:, slot * DC * 128 + dc * 128:
                                       slot * DC * 128 + (dc + 1) * 128],
                                   qtc[:, dc * 512:(dc + 1) * 512],
                                   start=(dc == 0), stop=(dc == DC - 1))
                mm.then_inc(ST.h, 1)
            prog.on("tensor", p3smm)
            t_mm = ST.inc()
            ktb_free[slot] = [(ST.h, t_mm)]

            def p3exp(s, kt=kt, bank=bank, w=t_mm,
                      pf=pexp_free if kt == 0 else ()):
                s.wait_ge(ST.h, w)
                _w(s, pf)
                _dr(s)
                s.activation(out=Pbuf[:, kt * 512:(kt + 1) * 512],
                             in_=pb[bank][:, 0:512], func=AF.Exp,
                             scale=1.0 / 32.0).then_inc(SA.h, 1)
                _dr(s)
            prog.on("scalar", p3exp)
            t_ex = SA.inc()
            bank_free[bank] = [(SA.h, t_ex)]

            def p3den(te, kt=kt, kt2=kt2, w=t_ex,
                      df=den_free if kt2 == 0 else ()):
                te.wait_ge(SA.h, w)
                _w(te, df)
                te.matmul(pb[6][0:1, 0:512], ones128[:],
                          Pbuf[:, kt * 512:(kt + 1) * 512],
                          start=(kt2 == 0), stop=(kt2 == NKT - 1)) \
                  .then_inc(ST.h, 1)
            prog.on("tensor", p3den)
            ST.inc()
        st["t_exp"][q4] = SA.v
        if q4 == NQ4 - 1:
            st["t_den"] = ST.v
            st["qdone"] = [(ST.h, ST.v)]

    def att_waves_mm(ac, q4):
        nonlocal pexp_free
        st = att[ac]
        for wave in range(2):
            for kt in range(QTL):
                kt2 = q4 * QTL + kt
                slot = kt % 2
                w_v = qload.issue(vtb[:, slot * 512:(slot + 1) * 512],
                                  v_s[kt2 * 128:(kt2 + 1) * 128,
                                      wave * 512:(wave + 1) * 512],
                                  waits=list(vtb_free[slot]) +
                                  (p12_done if ac == 0 and q4 == 0 and
                                   wave == 0 and kt < 2 else []))

                def p3wmm(te, kt=kt, slot=slot, wv=w_v, wex=st["t_exp"][q4],
                          bfs=[list(bank_free[d]) for d in range(4)]
                          if kt == 0 else None):
                    te.wait_ge(wv[0], wv[1])
                    te.wait_ge(SA.h, wex)
                    if bfs is not None:
                        for bf in bfs:
                            _w(te, bf)
                    for dvt in range(4):
                        mm = te.matmul(pb[dvt][:, 0:512],
                                       vtb[:, slot * 512 + dvt * 128:
                                           slot * 512 + (dvt + 1) * 128],
                                       Pbuf[:, kt * 512:(kt + 1) * 512],
                                       start=(kt == 0), stop=(kt == QTL - 1))
                    mm.then_inc(ST.h, 1)
                prog.on("tensor", p3wmm)
                t_mm = ST.inc()
                vtb_free[slot] = [(ST.h, t_mm)]
            t_wave = ST.v
            for dvt in range(4):
                dvg = wave * 4 + dvt

                def p3sse(s, dvt=dvt, dvg=dvg, w=t_wave, sf=sstg_free[dvg]):
                    s.wait_ge(ST.h, w)
                    _w(s, sf)
                    _dr(s)
                    s.activation(out=sstg[:, dvg * 512:(dvg + 1) * 512],
                                 in_=pb[dvt][:, 0:512],
                                 func=AF.Identity).then_inc(SA.h, 1)
                    _dr(s)
                prog.on("scalar", p3sse)
                st.setdefault("sev", {})[(q4, dvg)] = SA.inc()
            for dvt in range(4):
                bank_free[dvt] = [(SA.h, st["sev"][(q4, wave * 4 + dvt)])]
        pexp_free = [(ST.h, t_wave)]

    def att_waves_ev(ac, q4):
        st = att[ac]
        p = ac % 2
        for dvg in range(8):

            def p3sev(v, dvg=dvg, first=(q4 == 0), w=st["sev"][(q4, dvg)], p=p,
                      sf=sacc_free[p] if q4 == 0 and dvg == 0 else ()):
                v.wait_ge(SA.h, w)
                _w(v, sf)
                _dr(v)
                if first:
                    v.tensor_copy(out=s_acc[p][:, dvg * 512:(dvg + 1) * 512],
                                  in_=sstg[:, dvg * 512:(dvg + 1) * 512]) \
                        .then_inc(SV.h, 1)
                else:
                    v.tensor_add(out=s_acc[p][:, dvg * 512:(dvg + 1) * 512],
                                 in0=s_acc[p][:, dvg * 512:(dvg + 1) * 512],
                                 in1=sstg[:, dvg * 512:(dvg + 1) * 512]) \
                        .then_inc(SV.h, 1)
                _dr(v)
            prog.on("vector", p3sev)
            sstg_free[dvg] = [(SV.h, SV.inc())]

    def att_rec(ac):
        nonlocal den_free
        st = att[ac]
        p = ac % 2

        def p3rec(v, w=st["t_den"], rf=rec_free[p], dn=den_p[p], rc=rcp_p[p]):
            v.wait_ge(ST.h, w)
            _w(v, rf)
            _dr(v)
            v.tensor_copy(out=dn, in_=pb[6][0:1, 0:512])
            _dr(v)
            v.reciprocal(out=rc, in_=dn).then_inc(SV.h, 1)
            _dr(v)
        prog.on("vector", p3rec)
        st["t_rc"] = SV.inc()
        den_free = [(SV.h, st["t_rc"])]

    def ffn1_group(c, q4, wx8):
        last_mm = 0
        for fi in range(NG):
            f1t = q4 * NG + fi
            slot = f1t % 2
            w_w1 = qload.issue(w1qb[:, slot * DC * 128:(slot + 1) * DC * 128],
                               w1q_s[f1t], waits=list(w1qb_free[slot]) +
                               (p12_done if c == 0 and f1t < 2 else []))
            bank = 4 + f1t % 2

            def p3f1(te, slot=slot, bank=bank, ww=w_w1, wx=wx8,
                     bf=bank_free[bank], hf=h8_free if f1t == 0 else ()):
                te.wait_ge(ww[0], ww[1])
                te.wait_ge(wx[0], wx[1])
                _w(te, bf)
                _w(te, hf)
                if USE_DR:
                    for kp in range(DC // 2):
                        mm = te.matmul(
                            pb[bank][:, 0:512],
                            w1qb[:, slot * DC * 128 + kp * 256:
                                 slot * DC * 128 + (kp + 1) * 256]
                            .rearrange("p (two f) -> p two f", two=2),
                            xn8c[:, kp * 1024:(kp + 1) * 1024]
                            .rearrange("p (two t) -> p two t", two=2),
                            start=(kp == 0), stop=(kp == DC // 2 - 1),
                            perf_mode=mybir.MatmulPerfMode.DoubleRow)
                else:
                    for kc in range(DC):
                        mm = te.matmul(
                            pb[bank][:, 0:512],
                            w1qb[:, slot * DC * 128 + kc * 128:
                                 slot * DC * 128 + (kc + 1) * 128],
                            xn8c[:, kc * 512:(kc + 1) * 512],
                            start=(kc == 0), stop=(kc == DC - 1))
                mm.then_inc(ST.h, 1)
            prog.on("tensor", p3f1)
            t_mm = ST.inc()
            last_mm = t_mm
            w1qb_free[slot] = [(ST.h, t_mm)]
            yslot = f1t % 2

            def p3f1e(s, f1t=f1t, bank=bank, yslot=yslot, w=t_mm, wb=w_b1q,
                      yf=y1_free[yslot]):
                s.wait_ge(ST.h, w)
                s.wait_ge(wb[0], wb[1])
                _w(s, yf)
                _dr(s)
                s.activation(out=y1st[:, yslot * 512:(yslot + 1) * 512],
                             in_=pb[bank][:, 0:512], func=AF.Relu,
                             bias=b1_t[:, f1t:f1t + 1]).then_inc(SA.h, 1)
                _dr(s)
            prog.on("scalar", p3f1e)
            t_ev = SA.inc()
            bank_free[bank] = [(SA.h, t_ev)]
            if f1t % 2 == 1:
                items = [{"in": y1st[:, j * 512:(j + 1) * 512],
                          "out": h8[:, (f1t - 1 + j) * 512:(f1t + j) * 512]}
                         for j in range(2)]
                outs = emit_quantize(bld, items, M=2, emin=-14, signed=False,
                                     pre=[(SA.h, t_ev)], band=WBAND,
                                     slack=SLACK_H, ulev=5)
                y1_free[0] = [outs[0]]
                y1_free[1] = [outs[1]]
        return last_mm

    def ffn2(c):
        nonlocal h8_free, rb_free
        t_h8 = (SV.h, SV.v)
        g_marks = []
        prev_ev = None
        for f2t in range(DC):
            slot = f2t % 2
            w_w2 = qload.issue(
                w2qb[:, slot * NF1T * 128:(slot + 1) * NF1T * 128]
                .rearrange("p (f j) -> p f j", f=NF1T),
                w2q_s[:, :, f2t * 128:(f2t + 1) * 128]
                .rearrange("f p j -> p f j"),
                waits=list(w2qb_free[slot]) +
                (p12_done if c == 0 and f2t < 2 else []))

            def p3f2(te, ww=w_w2, slot=slot, wh=t_h8, pe=prev_ev,
                     rbf=rb_free if f2t == 0 else ()):
                te.wait_ge(ww[0], ww[1])
                te.wait_ge(wh[0], wh[1])
                if pe is not None:
                    te.wait_ge(SA.h, pe)
                _w(te, rbf)
                if USE_DR:
                    for fp in range(NF1T // 2):
                        mm = te.matmul(
                            pb[7][:, 0:512],
                            w2qb[:, slot * NF1T * 128 + fp * 256:
                                 slot * NF1T * 128 + (fp + 1) * 256]
                            .rearrange("p (two f) -> p two f", two=2),
                            h8[:, fp * 1024:(fp + 1) * 1024]
                            .rearrange("p (two t) -> p two t", two=2),
                            start=(fp == 0), stop=(fp == NF1T // 2 - 1),
                            perf_mode=mybir.MatmulPerfMode.DoubleRow)
                else:
                    for fc in range(NF1T):
                        mm = te.matmul(
                            pb[7][:, 0:512],
                            w2qb[:, slot * NF1T * 128 + fc * 128:
                                 slot * NF1T * 128 + (fc + 1) * 128],
                            h8[:, fc * 512:(fc + 1) * 512],
                            start=(fc == 0), stop=(fc == NF1T - 1))
                mm.then_inc(ST.h, 1)
            prog.on("tensor", p3f2)
            t_mm = ST.inc()
            w2qb_free[slot] = [(ST.h, t_mm)]
            yslot = f2t % 2

            def p3f2e(s, f2t=f2t, yslot=yslot, w=t_mm, wb=w_b2q,
                      yf=y2_free[yslot]):
                s.wait_ge(ST.h, w)
                s.wait_ge(wb[0], wb[1])
                _w(s, yf)
                _dr(s)
                s.activation(out=y2st[:, yslot * 512:(yslot + 1) * 512],
                             in_=pb[7][:, 0:512], func=AF.Identity,
                             bias=b2_t[:, f2t:f2t + 1]).then_inc(SA.h, 1)
                _dr(s)
            prog.on("scalar", p3f2e)
            prev_ev = SA.inc()
            if f2t % 2 == 1:
                items = [{"in": y2st[:, j * 512:(j + 1) * 512],
                          "out": y2st[:, j * 512:(j + 1) * 512],
                          "f2": f2t - 1 + j} for j in range(2)]
                ws = emit_quantize(bld, items, M=7, emin=-20, signed=True,
                                   pre=[(SA.h, prev_ev)], slack=SLACK_Y2,
                                   ulev=6)
                for j, it in enumerate(items):
                    f2_ = it["f2"]
                    mslot = f2_ % 2
                    w_mk = qload.issue(
                        mkb[:, mslot * 512:(mslot + 1) * 512],
                        maskT[c].rearrange("(dc p) t -> p dc t", p=128)[:, f2_, :],
                        waits=mk_free[mslot])

                    def p3mk(v, it=it, mslot=mslot, w=ws[j], wm=w_mk):
                        v.wait_ge(w[0], w[1])
                        v.wait_ge(wm[0], wm[1])
                        _dr(v)
                        v.tensor_mul(out=it["out"], in0=it["out"],
                                     in1=mkb[:, mslot * 512:(mslot + 1) * 512]) \
                            .then_inc(SV.h, 1)
                        _dr(v)
                    prog.on("vector", p3mk)
                    t_m = SV.inc()
                    mk_free[mslot] = [(SV.h, t_m)]

                    def p3si(s, it=it, f2_=f2_, w=t_m, gf=g_free):
                        s.wait_ge(SV.h, w)
                        _w(s, gf)
                        _dr(s)
                        s.activation(out=gbuf[:, f2_ * 512:(f2_ + 1) * 512],
                                     in_=it["out"],
                                     func=AF.Silu).then_inc(SA.h, 1)
                        _dr(s)
                    prog.on("scalar", p3si)
                    g_marks.append(SA.inc())
                y2_free[0] = [(SA.h, g_marks[-2])]
                y2_free[1] = [(SA.h, g_marks[-1])]
        h8_free = [(ST.h, ST.v)]
        return g_marks[-1], prev_ev

    if PIPELINE:
        att_start(0)
        for q4 in range(NQ4):
            if q4 > 0:
                att_waves_mm(0, q4 - 1)
            att_quarter(0, q4)
            if q4 > 0:
                att_waves_ev(0, q4 - 1)
        att_waves_mm(0, NQ4 - 1)
        att_waves_ev(0, NQ4 - 1)
        att_rec(0)

    for c in range(NP3):
        cn = c + 1 if PIPELINE else c
        w_x8 = qload.issue(xn8c[:], xn8_s[c],
                           waits=list(x8r_free) + [xn8_spill[c]] +
                           (list(x8_free) if c == 0 else []))
        if c == 0:
            x8_free = ()
        w_xc = qload.issue(xcb[:].rearrange("p (dc t) -> p dc t", dc=DC),
                           xT[c].rearrange("(dc p) t -> p dc t", p=128),
                           waits=list(xc_free) + (list(xcb_free) if c == 0 else []))
        if c == 0:
            xcb_free = ()
        if cn < NP3:
            prev = cn - 1
            if prev in att:
                qtc_free = att[prev]["qdone"]
            att_start(cn)
        last_f1 = 0
        for q4 in range(NQ4):
            if PIPELINE:
                last_f1 = ffn1_group(c, q4, w_x8)
            if cn < NP3:
                if q4 > 0:
                    att_waves_mm(cn, q4 - 1)
                att_quarter(cn, q4)
            if cn < NP3 and q4 > 0:
                att_waves_ev(cn, q4 - 1)
        if cn < NP3:
            att_waves_mm(cn, NQ4 - 1)
            att_waves_ev(cn, NQ4 - 1)
            att_rec(cn)
        if not PIPELINE:
            for q4 in range(NQ4):
                last_f1 = ffn1_group(c, q4, w_x8)
        x8r_free = [(ST.h, last_f1)]
        t_gl, ev7 = ffn2(c)
        p = c % 2

        def p3rb(te, w=att[c]["t_rc"], e7=ev7, rb=rb_free, rc=rcp_p[p]):
            te.wait_ge(SV.h, w)
            te.wait_ge(SA.h, e7)
            _w(te, rb)
            te.matmul(pb[7][:, 0:512], ones1[:], rc, start=True, stop=True) \
              .then_inc(ST.h, 1)
        prog.on("tensor", p3rb)
        t_rb = ST.inc()
        rec_free[p] = [(ST.h, t_rb)]

        for f2t in range(DC):
            def p3cb(v, f2t=f2t, wg=t_gl, wrb=t_rb, wx=w_xc, p=p):
                v.wait_ge(SA.h, wg)
                v.wait_ge(ST.h, wrb)
                v.wait_ge(wx[0], wx[1])
                _dr(v)
                v.tensor_mul(out=s_acc[p][:, f2t * 512:(f2t + 1) * 512],
                             in0=s_acc[p][:, f2t * 512:(f2t + 1) * 512],
                             in1=pb[7][:, 0:512])
                _dr(v)
                v.tensor_mul(out=s_acc[p][:, f2t * 512:(f2t + 1) * 512],
                             in0=s_acc[p][:, f2t * 512:(f2t + 1) * 512],
                             in1=gbuf[:, f2t * 512:(f2t + 1) * 512])
                _dr(v)
                v.tensor_add(out=xcb[:, f2t * 512:(f2t + 1) * 512],
                             in0=xcb[:, f2t * 512:(f2t + 1) * 512],
                             in1=s_acc[p][:, f2t * 512:(f2t + 1) * 512]) \
                    .then_inc(SV.h, 1)
                _dr(v)
            prog.on("vector", p3cb)
            SV.inc()
        t_cb = SV.v
        rb_free = [(SV.h, t_cb)]
        g_free = [(SV.h, t_cb)]
        sacc_free[p] = [(SV.h, t_cb)]
        w_out = qstore.issue(outT[c].rearrange("(dc p) t -> p dc t", p=128),
                             xcb[:].rearrange("p (dc t) -> p dc t", dc=DC),
                             waits=[(SV.h, t_cb)])
        xc_free = [w_out]

    def fin(gq, waits=tuple(qstore.all_done() + qspill.all_done())):
        _w(gq, waits)
    prog.on("gpsimd", fin)

    with nc.Block() as block:
        @block.sync
        def _(s):
            for fn in prog.items["sync"]:
                fn(s)

        @block.scalar
        def _(s):
            for fn in prog.items["scalar"]:
                fn(s)

        @block.vector
        def _(v):
            for fn in prog.items["vector"]:
                fn(v)

        @block.gpsimd
        def _(g):
            for fn in prog.items["gpsimd"]:
                fn(g)

        @block.tensor
        def _(te):
            for fn in prog.items["tensor"]:
                fn(te)

    es.close()
    return nc



def _tf2x32(k0, k1, x0, x1):
    def rotl(x, d):
        return ((x << np.uint32(d)) | (x >> np.uint32(32 - d))).astype(np.uint32)
    ks0, ks1 = np.uint32(k0), np.uint32(k1)
    ks2 = np.uint32(ks0 ^ ks1 ^ np.uint32(0x1BD11BDA))
    x0 = (x0 + ks0).astype(np.uint32)
    x1 = (x1 + ks1).astype(np.uint32)
    r1, r2 = (13, 15, 26, 6), (17, 29, 16, 24)
    ks = [ks0, ks1, ks2]
    for i, rots in enumerate((r1, r2, r1, r2, r1)):
        for d in rots:
            x0 = (x0 + x1).astype(np.uint32)
            x1 = rotl(x1, d)
            x1 = (x0 ^ x1).astype(np.uint32)
        x0 = (x0 + ks[(i + 1) % 3]).astype(np.uint32)
        x1 = (x1 + ks[(i + 2) % 3] + np.uint32(i + 1)).astype(np.uint32)
    return x0, x1


def _bernoulli_keep(seed, p, shape):
    n = int(np.prod(shape))
    i = np.arange(n, dtype=np.uint64)
    c1 = (i >> np.uint64(32)).astype(np.uint32)
    c2 = (i & np.uint64(0xFFFFFFFF)).astype(np.uint32)
    b1, b2 = _tf2x32(0, seed, c1, c2)
    import os
    fp64_regime = os.environ.get("BASS_SIM_FP64") == "1"
    try:
        import jax
        fp64_regime = fp64_regime or bool(jax.config.read("jax_enable_x64"))
    except Exception:
        pass
    if fp64_regime:
        bits = (b1.astype(np.uint64) << np.uint64(32)) | b2.astype(np.uint64)
        fb = ((bits >> np.uint64(12)) | np.uint64(0x3FF0000000000000)) \
            .view(np.float64) - 1.0
        return (fb < np.float64(p)).reshape(shape)
    bits = b1 ^ b2
    fb = ((bits >> np.uint32(9)) | np.uint32(0x3F800000)).view(np.float32) \
        - np.float32(1.0)
    return (fb < np.float32(p)).reshape(shape)


def _chunked_T(a):
    T, D = a.shape
    aT = np.ascontiguousarray(a.T)
    return np.ascontiguousarray(aT.reshape(D, T // 512, 512).transpose(1, 0, 2))


def kernel(L0_attn_t, L0_attn_W_qkv, L0_attn_b_qkv, L0_ffn_W1, L0_ffn_b1,
           L0_ffn_W2, L0_ffn_b2):
    x = np.asarray(L0_attn_t)
    B, T, D = x.shape
    F1 = np.asarray(L0_ffn_W1).shape[0]
    cfg = Cfg(T=T, F1=F1)
    nc = build_nc(cfg)

    keep = _bernoulli_keep(42, 0.8, (B, T, D))
    maskscale = (keep.astype(np.float32) * np.float32(1.25)).astype(
        ml_dtypes.bfloat16)

    wqkvT = np.ascontiguousarray(np.asarray(L0_attn_W_qkv).T)
    b_qkv = np.asarray(L0_attn_b_qkv)
    bqk = np.ascontiguousarray(b_qkv[:2 * D].reshape(16, 128).T)
    bvrow = np.ascontiguousarray(b_qkv[2 * D:][None, :])
    w1T = np.ascontiguousarray(np.asarray(L0_ffn_W1).T)
    w2T = np.ascontiguousarray(np.asarray(L0_ffn_W2).T)
    b1r = np.ascontiguousarray(np.asarray(L0_ffn_b1).reshape(-1, 128).T)
    b2r = np.ascontiguousarray(np.asarray(L0_ffn_b2).reshape(-1, 128).T)

    in_maps = []
    for b in range(B):
        in_maps.append({
            "xT": _chunked_T(x[b]), "wqkvT": wqkvT, "bqk": bqk, "bvrow": bvrow,
            "w1T": w1T, "w2T": w2T, "b1r": b1r, "b2r": b2r,
            "maskT": _chunked_T(maskscale[b]),
        })
    res = run_bass_kernel_spmd(nc, in_maps, core_ids=list(range(B)))
    o0 = np.asarray(res.results[0]["outT"])
    out = np.empty((B, T, D), dtype=o0.dtype)
    for b in range(B):
        o = np.asarray(res.results[b]["outT"])
        out[b] = o.transpose(1, 0, 2).reshape(D, T).T
    return out
